# version 04
# baseline (speedup 1.0000x reference)
import numpy as np
import concourse.bass as bass
import concourse.mybir as mybir
from concourse.bass_utils import run_bass_kernel_spmd
from contextlib import ExitStack

F32 = mybir.dt.float32
BF16 = mybir.dt.bfloat16
AF = mybir.ActivationFunctionType
ALU = mybir.AluOpType
AXX = mybir.AxisListType.X

B, S, D = 4, 4096, 1024
TQ = 2048
NQC = 16
HKC = 16
QT = 1024
NQT = 2
QCPT = 8
E, FF = 8, 4096
MC = FF // 128
KC = D // 128
SHIFT = -12.0
NEGM = -1e30
EPS = 1e-5

PV_ITEMS = [("bq", 256), ("bqc", 896), ("bqr", 128), ("bkv", 256), ("bkr", 128),
            ("bkc", 896), ("bv", 1024), ("bo", 1024), ("bdkv", 512), ("buk", 1024),
            ("buv", 1024), ("bdq", 512), ("buq", 1024), ("bo2", 1024),
            ("g1", 1024), ("be1", 1024), ("g2", 1024), ("be2", 1024)]
PV_COL = {}
_c = 0
for _n, _l in PV_ITEMS:
    PV_COL[_n] = _c
    _c += (_l + 127) // 128
PV_NCOL = _c


BF16_ONES_PAIR = 1.0019378662109375


def relayout_lhsT(W):
    K, M = W.shape
    kc = K // 128
    return np.ascontiguousarray(W.reshape(kc, 128, M).transpose(1, 0, 2)).reshape(128, kc * M)


def relayout_w1(W1):
    W = W1.reshape(E, KC, 128, MC // 2, 2, 128)
    return np.ascontiguousarray(W.transpose(0, 3, 2, 1, 4, 5))


def relayout_w2(W2):
    W = W2.reshape(E, MC, 128, KC, 128)
    return np.ascontiguousarray(W.transpose(0, 3, 2, 1, 4))


class BytePool:
    def __init__(self, nbytes):
        self.free = [(0, nbytes)]
        self.hwm = 0
        self.live = {}

    def alloc(self, nb, tag="", high=False):
        nb = (nb + 3) & ~3
        rng = list(enumerate(self.free))
        if high:
            for i, (a, b) in reversed(rng):
                if b - a >= nb:
                    self.free[i] = (a, b - nb)
                    if self.free[i][0] == self.free[i][1]:
                        self.free.pop(i)
                    self._chk(b - nb, nb, tag)
                    self.live[b - nb] = (nb, tag)
                    return b - nb
        else:
            for i, (a, b) in rng:
                if b - a >= nb:
                    self.free[i] = (a + nb, b)
                    if self.free[i][0] == self.free[i][1]:
                        self.free.pop(i)
                    self.hwm = max(self.hwm, a + nb)
                    self.live[a] = (nb, tag)
                    return a
        raise MemoryError(f"arena OOM for {tag}: need {nb}B, free={self.free}, "
                          f"live={sorted((v[1], v[0]) for v in self.live.values())}")

    def _chk(self, a, nb, tag):
        for o, (n2, t2) in self.live.items():
            if a < o + n2 and o < a + nb:
                raise RuntimeError(f"arena overlap: {tag}@[{a},{a+nb}) with {t2}@[{o},{o+n2})")

    def freeb(self, a, nb):
        nb = (nb + 3) & ~3
        assert a in self.live, f"double free at {a}"
        self.live.pop(a, None)
        self.free.append((a, a + nb))
        self.free.sort()
        merged = []
        for seg in self.free:
            if merged and merged[-1][1] >= seg[0]:
                merged[-1] = (merged[-1][0], max(merged[-1][1], seg[1]))
            else:
                merged.append(list(seg))
        self.free = [tuple(x) for x in merged]


class Arena:
    def __init__(self, handle, pool, elsz):
        self.h = handle
        self.pool = pool
        self.elsz = elsz

    def alloc(self, ncols, tag="", high=False):
        off = self.pool.alloc(ncols * self.elsz, tag, high=high)
        return (off, ncols, self.elsz)

    def freer(self, r):
        self.pool.freeb(r[0], r[1] * r[2])

    def ap(self, r, p0=0, p1=128, c0=0, ncols=None):
        if ncols is None:
            ncols = r[1] - c0
        boff = r[0] + c0 * self.elsz
        if self.elsz == 2:
            return self.h[p0:p1, boff // 2: boff // 2 + ncols]
        ap = self.h[p0:p1, boff // 2: boff // 2 + 2 * ncols]
        return ap.bitcast(F32)


class Sched:
    def __init__(self, nc, es):
        self.nc = nc
        self.es = es
        self.prog = {k: [] for k in ("tensor", "vector", "scalar", "gpsimd", "sync")}
        self.sems = {}
        self.cnt = {}

    def sem(self, name):
        if name not in self.sems:
            self.sems[name] = self.es.enter_context(self.nc.semaphore(name))
            self.cnt[name] = 0
        return name

    def n(self, name):
        return self.cnt.get(name, 0)

    def emit(self, eng, meth, *a, waits=(), inc=None, incn=1, **kw):
        for s, v in waits:
            self.sem(s)
        if inc:
            self.sem(inc)
            self.cnt[inc] += incn
            if incn == 16:
                if not hasattr(self, "dmasems"):
                    self.dmasems = set()
                self.dmasems.add(inc)
        self.prog[eng].append((meth, a, kw, list(waits), inc, incn))
        return self.cnt[inc] if inc else None

    def barrier(self):
        self.nbar = getattr(self, "nbar", 0) + 1
        dmaw = [(s_, self.cnt[s_]) for s_ in sorted(getattr(self, "dmasems", set()))]
        self.emit("sync", "nop", waits=dmaw, inc="bar")
        for eng in ("tensor", "vector", "scalar", "gpsimd"):
            self.emit(eng, "nop", inc="bar")
        for eng in ("tensor", "vector", "scalar", "gpsimd", "sync"):
            self.wait(eng, ("bar", 5 * self.nbar))

    def wait(self, eng, *waits):
        self.emit(eng, "wait_only", waits=list(waits))

    def replay_all(self, block):
        def mk(key):
            def run(eng):
                for i, (meth, a, kw, waits, inc, incn) in enumerate(self.prog[key]):
                    for s, v in waits:
                        eng.wait_ge(self.sems[s], v)
                    if meth == "wait_only":
                        continue
                    if meth == "__dbg__":
                        cb, ap_ = a
                        eng.dbg_aps(cb, ap_)
                        continue
                    inst = getattr(eng, meth)(*a, **kw)
                    inst.annotate(f"{key}#{i}")
                    if inc:
                        inst.then_inc(self.sems[inc], incn)
            return run
        block.tensor(mk("tensor"))
        block.vector(mk("vector"))
        block.scalar(mk("scalar"))
        block.gpsimd(mk("gpsimd"))
        block.sync(mk("sync"))


def build():
    import os
    nc = bass.Bass("TRN2", target_bir_lowering=False, debug=bool(os.environ.get("KDBG")))

    din = {}
    def D_(name, shape):
        din[name] = nc.dram_tensor(name, list(shape), F32, kind="ExternalInput").ap()
    D_("xT_own", (D, TQ))
    D_("xT_all", (D, S))
    D_("cs_q", (2, 128, TQ))
    D_("cs_k", (2, 128, S))
    D_("mh0", (128, 128))
    D_("mh1", (128, 128))
    D_("noise_tm", (128, NQC * 8))
    D_("pvec", (128, PV_NCOL))
    D_("bg8", (1, 8))
    D_("bn8", (1, 8))
    D_("b2", (E, D))
    D_("b1p", (128, E * MC))
    for w, k, m in [("wq", 8, 256), ("wqc", 2, 896), ("wqr", 8, 128), ("wkv", 8, 256),
                    ("wkr", 8, 128), ("wkc", 2, 896), ("wv", 2, 1024), ("wo", 8, 1024),
                    ("wdkv", 8, 512), ("wdq", 8, 512), ("wuk", 4, 1024), ("wuv", 4, 1024),
                    ("wuq", 4, 1024), ("wo2", 8, 1024), ("wg", 8, 8), ("wn", 8, 8)]:
        D_(w + "_r", (128, k * m))
    din["w1r"] = nc.dram_tensor("w1r", [E, MC // 2, 128, KC, 2, 128], F32, kind="ExternalInput").ap()
    din["w2r"] = nc.dram_tensor("w2r", [E, KC, 128, MC, 128], F32, kind="ExternalInput").ap()
    out_d = nc.dram_tensor("out", [D, TQ], F32, kind="ExternalOutput").ap()
    dbg_kind = dict(kind="ExternalOutput") if os.environ.get("KDBGHW") else {}
    cc_in = nc.dram_tensor("cc_in", [512, TQ], BF16)
    cc_out = nc.dram_tensor("cc_out", [1024, TQ], BF16)
    y1_dram = nc.dram_tensor("y1_dram", [D, TQ], F32, **dbg_kind)
    u1_dbg = nc.dram_tensor("u1_dbg", [D, TQ], F32)
    o1_dbg = nc.dram_tensor("o1_dbg", [D, TQ], F32, **dbg_kind)
    att_dbg = nc.dram_tensor("att_dbg", [128, 16384], F32, **dbg_kind)
    u2_dram = nc.dram_tensor("u2_dram", [D, TQ], F32, **dbg_kind)
    rows_dram = nc.dram_tensor("rows_dram", [2, TQ], F32)
    x1bf_dram = nc.dram_tensor("x1bf_dram", [D, TQ], BF16, **dbg_kind)

    es = ExitStack()
    sch = Sched(nc, es)
    with es:
        ARh = es.enter_context(nc.sbuf_tensor([128, 106000], BF16))
        pool = BytePool(106000 * 2)
        a32 = Arena(ARh, pool, 4)
        a16 = Arena(ARh, pool, 2)
        P1 = es.enter_context(nc.psum_tensor([128, 1024], F32))
        P2 = es.enter_context(nc.psum_tensor([128, 1024], F32))
        P3 = es.enter_context(nc.psum_tensor([128, 1024], F32))
        P4 = es.enter_context(nc.psum_tensor([128, 1024], F32))
        block = es.enter_context(nc.Block())

        r_pvec = a32.alloc(PV_NCOL, "pvec")
        r_ones = a32.alloc(128, "ones")
        r_shift = a32.alloc(1, "shift")
        r_mh0 = a32.alloc(128, "mh0")
        r_mh1 = a32.alloc(128, "mh1")
        r_onesbf = a16.alloc(2, "onesbf")
        r_rows = a32.alloc(6144, "rows", high=True)

        pvec = lambda name, c, p0=0, p1=128: a32.ap(r_pvec, p0, p1, PV_COL[name] + c, 1)
        ones_c = a32.ap(r_ones, 0, 128, 0, 1)
        ones_r = a32.ap(r_ones, 0, 1, 0, 128)
        ones_r64 = a32.ap(r_ones, 64, 65, 0, 128)
        ones_r32 = a32.ap(r_ones, 32, 33, 0, 128)
        onesbf_c = a16.ap(r_onesbf, 0, 128, 0, 1)
        shiftv = a32.ap(r_shift, 0, 128, 0, 1)
        mh0 = a32.ap(r_mh0)
        mh1 = a32.ap(r_mh1)
        ROW = lambda r, slot, c0=0, n=2048: a32.ap(r, 0, 1, slot * 2048 + c0, n)

        sch.emit("sync", "dma_start", out=a32.ap(r_pvec), in_=din["pvec"], inc="ld", incn=16)
        sch.emit("sync", "dma_start", out=mh0, in_=din["mh0"], inc="ld", incn=16)
        sch.emit("sync", "dma_start", out=mh1, in_=din["mh1"], inc="ld", incn=16)
        sch.emit("gpsimd", "memset", a32.ap(r_ones), 1.0, inc="msk")
        sch.emit("gpsimd", "memset", shiftv, SHIFT, inc="msk")
        sch.emit("gpsimd", "memset", onesbf_c, 1.0, inc="msk")
        LD0 = sch.n("ld")
        MSK0 = sch.n("msk")

        wsb = {}
        def load_w(name, kc, m, arena_tag=None, high=False):
            r = a16.alloc(kc * m, arena_tag or name, high=high)
            wsb[name] = r
            sch.emit("gpsimd", "dma_start", out=a16.ap(r), in_=din[name],
                     inc="ldg", incn=16)
            return r
        def wtile(name, kci, m0, mw, M):
            r = wsb[name]
            return a16.ap(r, 0, 128, kci * M + m0, mw)

        r_x0 = a32.alloc(8 * 2048, "x0")
        r_x0bf = a16.alloc(8 * 2048, "x0bf")
        for c in range(8):
            sch.emit("sync", "dma_start", out=a32.ap(r_x0, 0, 128, c * 2048, 2048),
                     in_=din["xT_own"][c * 128:(c + 1) * 128, :], inc="ld", incn=16)
            sch.emit("gpsimd", "dma_start", out=a16.ap(r_x0bf, 0, 128, c * 2048, 2048),
                     in_=din["xT_own"][c * 128:(c + 1) * 128, :], inc="ldg", incn=16)
        load_w("wq_r", 8, 256)
        load_w("wqr_r", 8, 128)
        LDG_A = sch.n("ldg")
        LD_A = sch.n("ld")

        r_cq1 = a16.alloc(2 * TQ, "cq1")
        r_qr1 = a16.alloc(TQ, "qr1r")
        r_csq = a32.alloc(2 * TQ, "csq")
        r_krp = a32.alloc(2 * 512, "krp")
        r_rtmp = a32.alloc(2 * 512, "ropetmp")
        for t in range(2):
            sch.emit("sync", "dma_start", out=a32.ap(r_csq, 0, 128, t * TQ, TQ),
                     in_=din["cs_q"][t], inc="ld", incn=16)
        LD_A2 = sch.n("ld")
        for seg in range(4):
            for m in range(2):
                ps = P1[:, m * 512:(m + 1) * 512]
                for kc in range(KC):
                    w = []
                    if kc == 0:
                        w.append(("aev", sch.n("aev")))
                        if seg == 0 and m == 0:
                            w += [("ldg", LDG_A), ("ld", LD_A)]
                    sch.emit("tensor", "matmul", ps, wtile("wq_r", kc, m * 128, 128, 256),
                             a16.ap(r_x0bf, 0, 128, kc * 2048 + seg * 512, 512),
                             start=(kc == 0), stop=(kc == KC - 1), waits=w,
                             inc=("amm" if kc == KC - 1 else None))
                sch.emit("scalar", "activation", a16.ap(r_cq1, 0, 128, m * TQ + seg * 512, 512),
                         ps, AF.Gelu, bias=pvec("bq", m),
                         waits=[("amm", sch.n("amm"))], inc="aev")
            psq = P2[:, (seg % 2) * 512:(seg % 2) * 512 + 512]
            for kc in range(KC):
                w = []
                if kc == 0:
                    w.append(("rev", sch.n("rev")))
                sch.emit("tensor", "matmul", psq, wtile("wqr_r", kc, 0, 128, 128),
                         a16.ap(r_x0bf, 0, 128, kc * 2048 + seg * 512, 512),
                         start=(kc == 0), stop=(kc == KC - 1), waits=w,
                         inc=("amm" if kc == KC - 1 else None))
            sch.emit("scalar", "activation", a32.ap(r_krp, 0, 128, (seg % 2) * 512, 512), psq,
                     AF.Identity, bias=pvec("bqr", 0),
                     waits=[("amm", sch.n("amm")), ("rev", sch.n("rev"))], inc="kpe")
            x1a = a32.ap(r_krp, 0, 64, (seg % 2) * 512, 512)
            x2a = a32.ap(r_krp, 64, 128, (seg % 2) * 512, 512)
            cos_t = a32.ap(r_csq, 0, 64, seg * 512, 512)
            cos_b = a32.ap(r_csq, 64, 128, seg * 512, 512)
            sin_t64 = a32.ap(r_csq, 64, 128, TQ + seg * 512, 512)
            sin_b0 = a32.ap(r_csq, 0, 64, TQ + seg * 512, 512)
            ta = a32.ap(r_rtmp, 0, 64, 0, 512)
            tb = a32.ap(r_rtmp, 0, 64, 512, 512)
            tc = a32.ap(r_rtmp, 64, 128, 0, 512)
            td = a32.ap(r_rtmp, 64, 128, 512, 512)
            qo_t = a16.ap(r_qr1, 0, 64, seg * 512, 512)
            qo_b = a16.ap(r_qr1, 64, 128, seg * 512, 512)
            sch.emit("vector", "tensor_mul", ta, x1a, cos_t,
                     waits=[("kpe", sch.n("kpe")), ("ld", LD_A2)])
            sch.emit("vector", "tensor_mul", tb, x2a, sin_t64)
            sch.emit("vector", "tensor_sub", qo_t, ta, tb)
            sch.emit("vector", "tensor_mul", tc, x2a, cos_b)
            sch.emit("vector", "tensor_mul", td, x1a, sin_b0)
            sch.emit("vector", "tensor_add", qo_b, tc, td, inc="rev")
        a16.freer(r_x0bf)
        a32.freer(r_csq)
        a16.freer(wsb.pop("wq_r"))
        a16.freer(wsb.pop("wqr_r"))
        sch.barrier()
        load_w("wkv_r", 8, 256)
        load_w("wkr_r", 8, 128)
        load_w("wqc_r", 2, 896)
        load_w("wkc_r", 2, 896)
        load_w("wv_r", 2, 1024)
        LDG_A1 = sch.n("ldg")

        REVQ = sch.n("rev")
        r_ckv1 = a16.alloc(2 * S, "ckv1")
        r_kr1 = a16.alloc(S, "kr1r")
        r_xblk = a16.alloc(2 * 8 * 512, "xblk")
        r_csk = a32.alloc(2 * 2 * 512, "cskblk")
        for blk in range(8):
            bb = blk % 2
            cw = [("aev", sch.n("aev"))] if blk >= 2 else []
            for kc in range(KC):
                sch.emit("gpsimd", "dma_start",
                         out=a16.ap(r_xblk, 0, 128, bb * 4096 + kc * 512, 512),
                         in_=din["xT_all"][kc * 128:(kc + 1) * 128, blk * 512:(blk + 1) * 512],
                         waits=(cw if kc == 0 else []),
                         inc=("xba" if bb == 0 else "xbb"), incn=16)
            for t in range(2):
                sch.emit("sync", "dma_start",
                         out=a32.ap(r_csk, 0, 128, bb * 1024 + t * 512, 512),
                         in_=din["cs_k"][t, :, blk * 512:(blk + 1) * 512],
                         waits=(cw if t == 0 else []),
                         inc=("csa" if bb == 0 else "csb"), incn=16)
            xw = ("xba" if bb == 0 else "xbb"), 16 * 8 * (blk // 2 + 1)
            for m in range(2):
                ps = P1[:, m * 512:(m + 1) * 512]
                for kc in range(KC):
                    w = []
                    if kc == 0:
                        w.append(xw)
                        w.append(("aev", sch.n("aev")))
                        if blk == 0 and m == 0:
                            w.append(("ldg", LDG_A1))
                    sch.emit("tensor", "matmul", ps, wtile("wkv_r", kc, m * 128, 128, 256),
                             a16.ap(r_xblk, 0, 128, bb * 4096 + kc * 512, 512),
                             start=(kc == 0), stop=(kc == KC - 1), waits=w,
                             inc=("amm" if kc == KC - 1 else None))
                sch.emit("scalar", "activation",
                         a16.ap(r_ckv1, 0, 128, m * S + blk * 512, 512), ps,
                         AF.Gelu, bias=pvec("bkv", m),
                         waits=[("amm", sch.n("amm"))], inc="aev")
            psk = P2[:, (bb) * 512:(bb + 1) * 512]
            for kc in range(KC):
                w = []
                if kc == 0:
                    w.append(xw)
                    w.append(("rev", max(REVQ, sch.n("rev") - 1)))
                sch.emit("tensor", "matmul", psk, wtile("wkr_r", kc, 0, 128, 128),
                         a16.ap(r_xblk, 0, 128, bb * 4096 + kc * 512, 512),
                         start=(kc == 0), stop=(kc == KC - 1), waits=w,
                         inc=("amm" if kc == KC - 1 else None))
            sch.emit("scalar", "activation", a32.ap(r_krp, 0, 128, bb * 512, 512), psk,
                     AF.Identity, bias=pvec("bkr", 0),
                     waits=[("amm", sch.n("amm"))], inc="kpe")
            csw = ("csa" if bb == 0 else "csb"), 16 * 2 * (blk // 2 + 1)
            x1a = a32.ap(r_krp, 0, 64, bb * 512, 512)
            x2a = a32.ap(r_krp, 64, 128, bb * 512, 512)
            cos_t = a32.ap(r_csk, 0, 64, bb * 1024, 512)
            cos_b = a32.ap(r_csk, 64, 128, bb * 1024, 512)
            sin_t64 = a32.ap(r_csk, 64, 128, bb * 1024 + 512, 512)
            sin_b0 = a32.ap(r_csk, 0, 64, bb * 1024 + 512, 512)
            ta = a32.ap(r_rtmp, 0, 64, 0, 512)
            tb = a32.ap(r_rtmp, 0, 64, 512, 512)
            tc = a32.ap(r_rtmp, 64, 128, 0, 512)
            td = a32.ap(r_rtmp, 64, 128, 512, 512)
            kout_t = a16.ap(r_kr1, 0, 64, blk * 512, 512)
            kout_b = a16.ap(r_kr1, 64, 128, blk * 512, 512)
            sch.emit("vector", "tensor_mul", ta, x1a, cos_t, waits=[("kpe", sch.n("kpe")), csw])
            sch.emit("vector", "tensor_mul", tb, x2a, sin_t64)
            sch.emit("vector", "tensor_sub", kout_t, ta, tb)
            sch.emit("vector", "tensor_mul", tc, x2a, cos_b)
            sch.emit("vector", "tensor_mul", td, x1a, sin_b0)
            sch.emit("vector", "tensor_add", kout_b, tc, td, inc="rev")
        a16.freer(r_xblk)
        a32.freer(r_csk)
        a32.freer(r_krp)
        a32.freer(r_rtmp)
        a16.freer(wsb.pop("wkv_r"))
        a16.freer(wsb.pop("wkr_r"))
        REV_ALL = sch.n("rev")
        sch.barrier()

        DBG = {}
        r_out1 = a16.alloc(8 * TQ, "out1", high=True)
        r_kp = a16.alloc(S, "Kpair")
        r_vp = a16.alloc(32 * 132, "Vpair")
        DBG["vpA"] = r_vp
        r_qp = a16.alloc(TQ, "Qpair")
        r_pt = a16.alloc(2 * QT, "PT")
        r_lbc = a32.alloc(QT, "lbc")
        r_ftmp = a32.alloc(QT, "ftmp", high=True)

        def emit_attention(blkname, kv_gen, q_gen, bias_v_slice, out1_r):
            for hp in range(8):
                kv_gen(hp)
                _dbgv(hp)
                q_gen(hp)
                for head01 in (0, 1):
                    pass
                gg = sch.n("gge")
                fin0 = sch.n("fin")
                for qt in range(NQT):
                    j0 = qt * QCPT
                    seq = []
                    for half in (0, 1):
                        for kcc in range(0, j0 + QCPT):
                            if kcc < j0:
                                seq.append((half, kcc, 0, None))
                            else:
                                seq.append((half, kcc, kcc - j0, (mh0 if half == 0 else mh1)))
                    for head in (0, 1):
                        hb = 64 * head
                        av_plan = []
                        for (half, kcc, t, mask) in seq:
                            c0 = t * 128
                            segs = []
                            s0 = c0
                            while s0 < QT:
                                seg_end = min(((s0 // 512) + 1) * 512, QT)
                                segs.append((s0, seg_end))
                                s0 = seg_end
                            av_plan.append(segs)
                        last_for_bank = {}
                        for ci, segs in enumerate(av_plan):
                            for si, (a0, a1) in enumerate(segs):
                                last_for_bank[a0 // 512] = (ci, si)
                        first_av = {0: True, 1: True}
                        pexs = f"pex{head}"
                        smms = f"smm{head}"
                        for ci, (half, kcc, t, mask) in enumerate(seq):
                            c0 = t * 128
                            kcol = half * S // 2 + kcc * 128
                            vchunk = half * HKC + kcc
                            segs = av_plan[ci]
                            for si, (a0, a1) in enumerate(segs):
                                w = []
                                if si == 0:
                                    w.append(("sfree", sch.n("sfree")))
                                    if ci == 0:
                                        w.append(("lev", sch.n("lev")))
                                    if ci == 0 and head == 0:
                                        w.append(("gge", gg))
                                sch.emit("tensor", "matmul", P1[:, a0:a1],
                                         a16.ap(r_kp, hb, hb + 64, kcol, 128),
                                         a16.ap(r_qp, hb, hb + 64, qt * QT + a0, a1 - a0),
                                         start=True, stop=True, waits=w,
                                         inc=(smms if si == len(segs) - 1 else None))
                            if mask is not None:
                                sch.emit("vector", "tensor_tensor", P1[:, c0:c0 + 128],
                                         P1[:, c0:c0 + 128], mask, ALU.add,
                                         waits=[(smms, sch.n(smms)), ("ld", LD0)],
                                         inc="mvk")
                                expw = [("mvk", sch.n("mvk"))]
                            else:
                                expw = [(smms, sch.n(smms))]
                            sch.emit("scalar", "activation",
                                     a16.ap(r_pt, 0, 128, head * QT + c0, QT - c0),
                                     P1[:, c0:QT], AF.Exp, scale=0.125, bias=shiftv,
                                     waits=expw + ([("msk", MSK0)] if qt == 0 and ci == 0 else []),
                                     inc="sfree")
                            for si, (a0, a1) in enumerate(segs):
                                w = [("sfree", sch.n("sfree"))] if si == 0 else []
                                if first_av[a0 // 512] and fin0 > 0:
                                    w.append(("fin", fin0))
                                is_last = last_for_bank[a0 // 512] == (ci, si)
                                if head == 0:
                                    sch.emit("tensor", "matmul", P3[0:64, a0:a1],
                                             a16.ap(r_vp, 0, 128, vchunk * 132, 64),
                                             a16.ap(r_pt, 0, 128, head * QT + a0, a1 - a0),
                                             start=first_av[a0 // 512], stop=is_last, waits=w)
                                    sch.emit("tensor", "matmul", P4[0:1, a0:a1],
                                             onesbf_c,
                                             a16.ap(r_pt, 0, 128, head * QT + a0, a1 - a0),
                                             start=first_av[a0 // 512], stop=is_last,
                                             inc=("oav" if ci == len(seq) - 1 and si == len(segs) - 1 else None))
                                else:
                                    sch.emit("tensor", "matmul", P4[64:128, a0:a1],
                                             a16.ap(r_vp, 0, 128, vchunk * 132 + 66, 64),
                                             a16.ap(r_pt, 0, 128, head * QT + a0, a1 - a0),
                                             start=first_av[a0 // 512], stop=is_last, waits=w)
                                    sch.emit("tensor", "matmul", P2[0:1, a0:a1],
                                             onesbf_c,
                                             a16.ap(r_pt, 0, 128, head * QT + a0, a1 - a0),
                                             start=first_av[a0 // 512], stop=is_last,
                                             inc=("oav" if ci == len(seq) - 1 and si == len(segs) - 1 else None))
                                first_av[a0 // 512] = False
                        ch = hp
                        if head == 0:
                            lrow = ROW(r_rows, 0, 0, QT)
                            sch.emit("vector", "tensor_copy", lrow, P4[0:1, :],
                                     waits=[("oav", sch.n("oav"))], inc="lev")
                            sch.emit("vector", "reciprocal", lrow, lrow, inc="lev")
                            sch.wait("vector", ("lev", sch.n("lev")))
                            for sg in range(2):
                                sch.emit("tensor", "matmul", P1[:, sg * 512:(sg + 1) * 512],
                                         ones_r[:, 0:128], lrow[:, sg * 512:(sg + 1) * 512],
                                         start=True, stop=True,
                                         waits=([("lev", sch.n("lev"))] if sg == 0 else []),
                                         inc=("lbr" if sg == 1 else None))
                            sch.emit("scalar", "activation", a32.ap(r_lbc, 0, 64, 0, QT),
                                     P1[0:64, :], AF.Copy,
                                     waits=[("lbr", sch.n("lbr"))], inc="lev")
                            sch.emit("vector", "scalar_tensor_tensor",
                                     a32.ap(r_ftmp, 0, 64, 0, QT),
                                     P3[0:64, :], 0.0, a32.ap(r_lbc, 0, 64, 0, QT),
                                     ALU.add, ALU.mult,
                                     waits=[("lev", sch.n("lev"))], inc="lev")
                            sch.emit("scalar", "activation",
                                     a16.ap(out1_r, 0, 64, ch * TQ + qt * QT, QT),
                                     a32.ap(r_ftmp, 0, 64, 0, QT), AF.Identity,
                                     bias=bias_v_slice(2 * hp),
                                     waits=[("lev", sch.n("lev"))], inc="fin")
                            if os.environ.get("KDBG4") and blkname == "a1" and hp == 0 and qt == 0:
                                sch.emit("scalar", "activation",
                                         ROW(r_rows2, 0, 0, QT), P4[0:1, :], AF.Copy,
                                         waits=[("fin", sch.n("fin"))], inc="dbg4s")
                                sch.emit("sync", "dma_start", out=att_dbg[0:64, 0:1024],
                                         in_=a32.ap(r_ftmp, 0, 64, 0, QT),
                                         waits=[("dbg4s", 1), ("fin", sch.n("fin"))],
                                         inc="dbg4", incn=16)
                                sch.emit("sync", "dma_start", out=att_dbg[0:64, 1024:2048],
                                         in_=a32.ap(r_lbc, 0, 64, 0, QT),
                                         inc="dbg4", incn=16)
                                sch.emit("sync", "dma_start", out=att_dbg[0:1, 2048:3072],
                                         in_=a32.ap(r_rows2, 0, 1, 0, 1024),
                                         inc="dbg4", incn=16)
                                sch.emit("sync", "dma_start", out=att_dbg[0:1, 3072:4096],
                                         in_=a32.ap(r_rows, 0, 1, 0, 1024),
                                         inc="dbg4", incn=16)
                                sch.emit("scalar", "activation",
                                         a32.ap(r_rows2, 0, 64, 1024, 1024),
                                         P1[0:64, :], AF.Copy, inc="dbg4s")
                                sch.emit("sync", "dma_start",
                                         out=att_dbg[0:64, 4096:5120],
                                         in_=a32.ap(r_rows2, 0, 64, 1024, 1024),
                                         waits=[("dbg4s", 2)], inc="dbg4", incn=16)
                                sch.emit("sync", "wait_only", waits=[("dbg4", 80)])
                        else:
                            lrow = ROW(r_rows, 0, 0, QT)
                            sch.emit("vector", "tensor_copy", lrow, P2[0:1, :],
                                     waits=[("oav", sch.n("oav"))], inc="lev")
                            sch.emit("vector", "reciprocal", lrow, lrow, inc="lev")
                            sch.wait("vector", ("lev", sch.n("lev")))
                            for sg in range(2):
                                sch.emit("tensor", "matmul", P1[:, sg * 512:(sg + 1) * 512],
                                         ones_r[:, 0:128], lrow[:, sg * 512:(sg + 1) * 512],
                                         start=True, stop=True,
                                         waits=([("lev", sch.n("lev"))] if sg == 0 else []),
                                         inc=("lbr" if sg == 1 else None))
                            sch.emit("scalar", "activation", a32.ap(r_lbc, 64, 128, 0, QT),
                                     P1[64:128, :], AF.Copy,
                                     waits=[("lbr", sch.n("lbr"))], inc="lev")
                            sch.emit("vector", "scalar_tensor_tensor",
                                     a32.ap(r_ftmp, 64, 128, 0, QT),
                                     P4[64:128, :], 0.0, a32.ap(r_lbc, 64, 128, 0, QT),
                                     ALU.add, ALU.mult,
                                     waits=[("lev", sch.n("lev"))], inc="lev")
                            sch.emit("scalar", "activation",
                                     a16.ap(out1_r, 64, 128, ch * TQ + qt * QT, QT),
                                     a32.ap(r_ftmp, 64, 128, 0, QT), AF.Identity,
                                     bias=bias_v_slice(2 * hp + 1),
                                     waits=[("lev", sch.n("lev"))], inc="fin")
                    fin0 = sch.n("fin")
                    if os.environ.get("KDBG3") and blkname == "a1" and hp == 0 and qt == 0 and head == 1:
                        col = 0
                        for nm_, rr_, nn_ in [("pt", r_pt, 2048), ("kp", r_kp, 4096),
                                              ("qp", r_qp, 2048), ("vp", r_vp, 4224)]:
                            sch.emit("gpsimd", "dma_start",
                                     out=att_dbg[:, col:col + nn_],
                                     in_=a16.ap(rr_, 0, 128, 0, nn_),
                                     waits=[("fin", sch.n("fin"))], inc="dbg3g", incn=16)
                            col += nn_
                        for nm_, rr_ in [("lbc", r_lbc), ("ftmp", r_ftmp)]:
                            sch.emit("sync", "dma_start",
                                     out=att_dbg[:, col:col + 1024],
                                     in_=a32.ap(rr_, 0, 128, 0, 1024),
                                     waits=[("fin", sch.n("fin"))], inc="dbg3", incn=16)
                            col += 1024

        def kv1_gen(hp):
            _dbgv0 = hp
            if hp < 7:
                for kb in range(8):
                    pb = kb % 2
                    ps = P2[:, pb * 512:(pb + 1) * 512]
                    for lc in range(2):
                        w = []
                        if lc == 0:
                            w.append(("gge", sch.n("gge")))
                            if kb == 0:
                                w.append(("lev", sch.n("lev")))
                            if hp == 0 and kb == 0:
                                w.append(("fin", sch.n("fin")))
                        sch.emit("tensor", "matmul", ps,
                                 wtile("wkc_r", lc, hp * 128, 128, 896),
                                 a16.ap(r_ckv1, 0, 128, lc * S + kb * 512, 512),
                                 start=(lc == 0), stop=(lc == 1), waits=w,
                                 inc=("gmm" if lc == 1 else None))
                    sch.emit("scalar", "activation", a16.ap(r_kp, 0, 128, kb * 512, 512),
                             ps, AF.Gelu, bias=pvec("bkc", hp),
                             waits=[("gmm", sch.n("gmm"))], inc="gge")
            else:
                for kb in range(4):
                    sch.emit("scalar", "activation", a16.ap(r_kp, 0, 128, kb * 1024, 1024),
                             a16.ap(r_kr1, 0, 128, kb * 1024, 1024), AF.Gelu,
                             waits=[("rev", REV_ALL)] if kb == 0 else [], inc="gge")
            for j in range(32):
                pb = j % 2
                ps = P2[:, pb * 512: pb * 512 + 128]
                for lc in range(2):
                    w = [("gge", sch.n("gge"))] if lc == 0 else []
                    sch.emit("tensor", "matmul", ps,
                             a16.ap(r_ckv1, 0, 128, lc * S + j * 128, 128),
                             wtile("wv_r", lc, hp * 128, 128, 1024),
                             start=(lc == 0), stop=(lc == 1), waits=w,
                             inc=("gmm" if lc == 1 else None))
                sch.emit("scalar", "activation", a16.ap(r_vp, 0, 128, j * 132, 64),
                         ps[:, 0:64], AF.Copy,
                         waits=[("gmm", sch.n("gmm"))], inc="gge")
                sch.emit("scalar", "activation", a16.ap(r_vp, 0, 128, j * 132 + 66, 64),
                         ps[:, 64:128], AF.Copy, inc="gge")

        def _dbgv(hp):
            if not os.environ.get("KDBG"):
                return
            vchk = a16.ap(r_vp, 0, 128, 0, 4224)
            def cb(arr):
                import numpy as _np
                a = _np.asarray(arr, dtype=_np.float32)
                ones = a[:, 64::132]
                print(f"DBGV pair{hp}: ones==1 {(ones==1.0).sum()} bad {(ones!=1.0).sum()}",
                      flush=True)
            sch.prog["scalar"].append(("__dbg__", (cb, vchk), {}, [], None, 1))

        def q1_gen(hp):
            if hp < 7:
                for qb in range(4):
                    pb = qb % 2
                    ps = P2[:, pb * 512:(pb + 1) * 512]
                    for lc in range(2):
                        w = [("gge", sch.n("gge"))] if lc == 0 else []
                        sch.emit("tensor", "matmul", ps,
                                 wtile("wqc_r", lc, hp * 128, 128, 896),
                                 a16.ap(r_cq1, 0, 128, lc * TQ + qb * 512, 512),
                                 start=(lc == 0), stop=(lc == 1), waits=w,
                                 inc=("gmm" if lc == 1 else None))
                    sch.emit("scalar", "activation", a16.ap(r_qp, 0, 128, qb * 512, 512),
                             ps, AF.Gelu, bias=pvec("bqc", hp),
                             waits=[("gmm", sch.n("gmm"))], inc="gge")
            else:
                for qb in range(2):
                    sch.emit("scalar", "activation", a16.ap(r_qp, 0, 128, qb * 1024, 1024),
                             a16.ap(r_qr1, 0, 128, qb * 1024, 1024), AF.Gelu,
                             waits=[("rev", REV_ALL)] if qb == 0 else [], inc="gge")

        def bv_slice(h):
            return pvec("bv", h // 2, 64 * (h % 2), 64 * (h % 2) + 64)

        emit_attention("a1", kv1_gen, q1_gen, bv_slice, r_out1)
        a16.freer(r_ckv1)
        a16.freer(r_kr1)
        a16.freer(r_cq1)
        a16.freer(r_qr1)
        a16.freer(wsb.pop("wqc_r"))
        a16.freer(wsb.pop("wkc_r"))
        a16.freer(wsb.pop("wv_r"))
        a16.freer(r_kp)
        a16.freer(r_vp)
        a16.freer(r_qp)
        a16.freer(r_pt)
        a32.freer(r_lbc)
        sch.barrier()

        if os.environ.get("KDBG2"):
            for c_ in range(8):
                sch.emit("gpsimd", "dma_start", out=o1_dbg[c_ * 128:(c_ + 1) * 128, :],
                         in_=a16.ap(r_out1, 0, 128, c_ * TQ, TQ),
                         waits=[("fin", sch.n("fin"))], inc="o1d", incn=16)
        r_wo = load_w("wo_r", 8, 1024, high=True)
        r_sq = a32.alloc(512, "sqscratch", high=True)
        r_bcA = a32.alloc(2048, "bcA", high=True)
        r_bcB = a32.alloc(2048, "bcB", high=True)

        def ln_block(r_u, dst_write, gamma, beta, rowsr, rslot_persist=None,
                     wait_sem=None, dst16=None, dst16_dram=None, dram_f32=None):
            for cg in range(4):
                sl = slice(cg * 512, (cg + 1) * 512)
                ps1 = P3[0:1, cg * 256:cg * 256 + 512] if False else P3[0:1, 0:512]
                ps2 = P4[0:1, 0:512]
                for c in range(8):
                    w = []
                    if c == 0:
                        w.append((wait_sem[0], wait_sem[1]))
                        if cg > 0:
                            w.append(("lse", sch.n("lse")))
                    sch.emit("tensor", "matmul", ps1, ones_c,
                             a32.ap(r_u, 0, 128, c * 2048 + cg * 512, 512),
                             start=(c == 0), stop=(c == 7), waits=w,
                             inc=("lsm" if c == 7 else None))
                for c in range(8):
                    sch.emit("scalar", "activation", a32.ap(r_sq, 0, 128, 0, 512),
                             a32.ap(r_u, 0, 128, c * 2048 + cg * 512, 512), AF.Square,
                             waits=([("lsm", sch.n("lsm"))] if c == 0 else
                                    [("sqe", sch.n("sqe"))]), inc="sqp")
                    sch.emit("tensor", "matmul", ps2, ones_c, a32.ap(r_sq, 0, 128, 0, 512),
                             start=(c == 0), stop=(c == 7),
                             waits=[("sqp", sch.n("sqp"))],
                             inc=("lsm" if c == 7 else "sqe"))
                sch.emit("vector", "tensor_copy", ROW(r_rows, 0, cg * 512, 512), ps1,
                         waits=[("lsm", sch.n("lsm"))], inc="lse")
                sch.emit("vector", "tensor_copy", ROW(r_rows, 1, cg * 512, 512), ps2,
                         inc="lse")
            Arow = ROW(r_rows, 0)
            Brow = ROW(r_rows, 1)
            Crow = ROW(r_rows, 2)
            sch.wait("vector", ("lse", sch.n("lse")))
            sch.emit("vector", "tensor_scalar_mul", Arow, Arow, 1.0 / 1024)
            sch.emit("vector", "tensor_scalar_mul", Brow, Brow, 1.0 / 1024)
            sch.emit("vector", "tensor_mul", Crow, Arow, Arow)
            sch.emit("vector", "tensor_sub", Brow, Brow, Crow)
            sch.emit("vector", "tensor_scalar_add", Brow, Brow, EPS, inc="lse")
            sch.emit("scalar", "activation", Brow, Brow, AF.Sqrt,
                     waits=[("lse", sch.n("lse"))], inc="lse")
            sch.wait("vector", ("lse", sch.n("lse")))
            sch.emit("vector", "reciprocal", Brow, Brow)
            sch.emit("vector", "scalar_tensor_tensor", Crow, Arow, -1.0, Brow,
                     ALU.mult, ALU.mult, inc="lse")
            if rslot_persist is not None:
                sch.emit("sync", "dma_start", out=rows_dram[0:1, :], in_=Brow,
                         waits=[("lse", sch.n("lse"))], inc="rwd", incn=16)
                sch.emit("sync", "dma_start", out=rows_dram[1:2, :], in_=Crow,
                         inc="rwd", incn=16)
            sch.wait("tensor", ("lse", sch.n("lse")))
            for sg in range(4):
                sch.emit("tensor", "matmul", P3[:, 0:512], ones_r,
                         Brow[:, sg * 512:(sg + 1) * 512], start=True, stop=True,
                         waits=[("bce", sch.n("bce"))] if sg > 0 else [])
                sch.emit("tensor", "matmul", P4[:, 0:512], ones_r,
                         Crow[:, sg * 512:(sg + 1) * 512], start=True, stop=True, inc="bcm")
                sch.emit("scalar", "activation", a32.ap(r_bcA, 0, 128, sg * 512, 512),
                         P3[:, 0:512], AF.Copy, waits=[("bcm", sch.n("bcm"))])
                sch.emit("scalar", "activation", a32.ap(r_bcB, 0, 128, sg * 512, 512),
                         P4[:, 0:512], AF.Copy, inc="bce")
            for c in range(8):
                uap = a32.ap(r_u, 0, 128, c * 2048, 2048)
                tap = a32.ap(r_sq, 0, 128, 0, 2048) if False else None
                dst_write(c, uap)
            return

        r_y1bf = a16.alloc(8 * TQ, "y1bf")

        def y1_write(c, uap):
            for cg in range(4):
                u = a32.ap(r_x0, 0, 128, c * 2048 + cg * 512, 512)
                A_ = a32.ap(r_bcA, 0, 128, cg * 512, 512)
                B_ = a32.ap(r_bcB, 0, 128, cg * 512, 512)
                t = a32.ap(r_sq, 0, 128, 0, 512)
                w = [("bce", sch.n("bce"))] if (c == 0 and cg == 0) else []
                sch.emit("vector", "tensor_mul", t, u, A_, waits=w)
                sch.emit("vector", "tensor_add", u, t, B_, inc="lna")
            sch.emit("sync", "dma_start", out=y1_dram[c * 128:(c + 1) * 128, :],
                     in_=a32.ap(r_x0, 0, 128, c * 2048, 2048),
                     waits=[("lna", sch.n("lna"))], inc="y1d", incn=16)
            sch.emit("scalar", "activation", a16.ap(r_y1bf, 0, 128, c * 2048, 2048),
                     a32.ap(r_x0, 0, 128, c * 2048, 2048), AF.Copy,
                     waits=[("lna", sch.n("lna"))], inc="y1b")

        for m in range(8):
            for sg in range(4):
                pb = (m * 4 + sg) % 2
                ps = (P1 if pb == 0 else P2)[:, 0:512]
                for kc in range(8):
                    w = []
                    if kc == 0:
                        w.append(("fin", sch.n("fin")))
                        w.append(("ldg", sch.n("ldg")))
                        if m * 4 + sg >= 2:
                            w.append(("pje", sch.n("pje") - 1))
                    sch.emit("tensor", "matmul", ps, wtile("wo_r", kc, m * 128, 128, 1024),
                             a16.ap(r_out1, 0, 128, kc * TQ + sg * 512, 512),
                             start=(kc == 0), stop=(kc == 7), waits=w,
                             inc=("pjm" if kc == 7 else None))
                sch.emit("scalar", "activation", a32.ap(r_ftmp, 0, 128, 0, 512), ps,
                         AF.Identity, bias=pvec("bo", m),
                         waits=[("pjm", sch.n("pjm")), ("pje", sch.n("pje"))], inc="pjs")
                sch.emit("vector", "tensor_add",
                         a32.ap(r_x0, 0, 128, m * 2048 + sg * 512, 512),
                         a32.ap(r_x0, 0, 128, m * 2048 + sg * 512, 512),
                         a32.ap(r_ftmp, 0, 128, 0, 512),
                         waits=[("pjs", sch.n("pjs"))], inc="pje")
        _lnw = ("pje", sch.n("pje"))
        if os.environ.get("KDBG2"):
            for c_ in range(8):
                sch.emit("sync", "dma_start", out=u1_dbg[c_ * 128:(c_ + 1) * 128, :],
                         in_=a32.ap(r_x0, 0, 128, c_ * 2048, 2048),
                         waits=[("pje", sch.n("pje"))], inc="u1d", incn=16)
            _lnw = ("u1d", sch.n("u1d"))
        ln_block(r_x0, y1_write, None, None, r_rows, wait_sem=_lnw)
        a16.freer(r_out1)
        a32.freer(r_x0)
        a16.freer(wsb.pop("wo_r"))
        sch.barrier()

        r_wdkv = load_w("wdkv_r", 8, 512)
        r_wdq = load_w("wdq_r", 8, 512)
        LDG_B = sch.n("ldg")
        r_ckv2o = a16.alloc(4 * TQ, "ckv2own")
        r_cq2 = a16.alloc(4 * TQ, "cq2")
        for (nm, dstr, bias, osem) in [("wdkv_r", r_ckv2o, "bdkv", "c2e"), ("wdq_r", r_cq2, "bdq", "c2e")]:
            for m in range(4):
                for sg in range(4):
                    pb = (m * 4 + sg) % 2
                    ps = (P1 if pb == 0 else P2)[:, 0:512]
                    for kc in range(8):
                        w = []
                        if kc == 0:
                            w.append(("y1b", sch.n("y1b")))
                            w.append(("ldg", LDG_B))
                            w.append(("pje", sch.n("pje")))
                            if sch.n("c2e") >= 2:
                                w.append(("c2e", sch.n("c2e") - 1))
                        sch.emit("tensor", "matmul", ps, wtile(nm, kc, m * 128, 128, 512),
                                 a16.ap(r_y1bf, 0, 128, kc * 2048 + sg * 512, 512),
                                 start=(kc == 0), stop=(kc == 7), waits=w,
                                 inc=("c2m" if kc == 7 else None))
                    sch.emit("scalar", "activation",
                             a16.ap(dstr, 0, 128, m * TQ + sg * 512, 512), ps,
                             AF.Identity, bias=pvec(bias, m),
                             waits=[("c2m", sch.n("c2m"))], inc="c2e")
        for m in range(4):
            sch.emit("sync", "dma_start", out=cc_in[m * 128:(m + 1) * 128, :],
                     in_=a16.ap(r_ckv2o, 0, 128, m * TQ, TQ),
                     waits=[("c2e", 4 * (m + 1))], inc="cci", incn=16)
        sch.emit("gpsimd", "collective_compute", "AllGather", ALU.bypass,
                 replica_groups=[[0, 1], [2, 3], [4, 5], [6, 7]],
                 ins=[cc_in.ap().opt()], outs=[cc_out.ap().opt()],
                 waits=[("cci", sch.n("cci"))], inc="ccd")
        r_ckv2f = a16.alloc(4 * S, "ckv2f")
        for m in range(4):
            for e_ in range(2):
                sch.emit("sync", "dma_start",
                         out=a16.ap(r_ckv2f, 0, 128, m * S + e_ * TQ, TQ),
                         in_=cc_out[e_ * 512 + m * 128: e_ * 512 + m * 128 + 128, :],
                         waits=[("ccd", 1)], inc="ccr", incn=16)
        a16.freer(r_ckv2o)
        sch.wait("gpsimd", ("cci", sch.n("cci")))
        r_wuk = load_w("wuk_r", 4, 1024)
        r_wuv = load_w("wuv_r", 4, 1024)
        r_wuq = load_w("wuq_r", 4, 1024)
        LDG_B2 = sch.n("ldg")
        r_kp = a16.alloc(S, "Kpair")
        r_vp = a16.alloc(32 * 132, "Vpair")
        r_qp = a16.alloc(TQ, "Qpair")
        r_pt = a16.alloc(2 * QT, "PT")
        r_lbc = a32.alloc(QT, "lbc")

        def kv2_gen(hp):
            for kb in range(8):
                pb = kb % 2
                ps = P2[:, pb * 512:(pb + 1) * 512]
                for lc in range(4):
                    w = []
                    if lc == 0:
                        w.append(("gge", sch.n("gge")))
                        if kb == 0:
                            w.append(("lev", sch.n("lev")))
                        if hp == 0 and kb == 0:
                            w += [("ccr", 16 * 8), ("ldg", LDG_B2), ("fin", sch.n("fin"))]
                    sch.emit("tensor", "matmul", ps,
                             wtile("wuk_r", lc, hp * 128, 128, 1024),
                             a16.ap(r_ckv2f, 0, 128, lc * S + kb * 512, 512),
                             start=(lc == 0), stop=(lc == 3), waits=w,
                             inc=("gmm" if lc == 3 else None))
                sch.emit("scalar", "activation", a16.ap(r_kp, 0, 128, kb * 512, 512),
                         ps, AF.Identity, bias=pvec("buk", hp),
                         waits=[("gmm", sch.n("gmm"))], inc="gge")
            for j in range(32):
                pb = j % 2
                ps = P2[:, pb * 512: pb * 512 + 128]
                for lc in range(4):
                    w = [("gge", sch.n("gge"))] if lc == 0 else []
                    sch.emit("tensor", "matmul", ps,
                             a16.ap(r_ckv2f, 0, 128, lc * S + j * 128, 128),
                             wtile("wuv_r", lc, hp * 128, 128, 1024),
                             start=(lc == 0), stop=(lc == 3), waits=w,
                             inc=("gmm" if lc == 3 else None))
                sch.emit("scalar", "activation", a16.ap(r_vp, 0, 128, j * 132, 64),
                         ps[:, 0:64], AF.Copy,
                         waits=[("gmm", sch.n("gmm"))], inc="gge")
                sch.emit("scalar", "activation", a16.ap(r_vp, 0, 128, j * 132 + 66, 64),
                         ps[:, 64:128], AF.Copy, inc="gge")

        def q2_gen(hp):
            for qb in range(4):
                pb = qb % 2
                ps = P2[:, pb * 512:(pb + 1) * 512]
                for lc in range(4):
                    w = [("gge", sch.n("gge"))] if lc == 0 else []
                    sch.emit("tensor", "matmul", ps,
                             wtile("wuq_r", lc, hp * 128, 128, 1024),
                             a16.ap(r_cq2, 0, 128, lc * TQ + qb * 512, 512),
                             start=(lc == 0), stop=(lc == 3), waits=w,
                             inc=("gmm" if lc == 3 else None))
                sch.emit("scalar", "activation", a16.ap(r_qp, 0, 128, qb * 512, 512),
                         ps, AF.Identity, bias=pvec("buq", hp),
                         waits=[("gmm", sch.n("gmm"))], inc="gge")

        def buv_slice(h):
            return pvec("buv", h // 2, 64 * (h % 2), 64 * (h % 2) + 64)

        a16.freer(r_y1bf)
        a16.freer(wsb.pop("wdkv_r"))
        a16.freer(wsb.pop("wdq_r"))
        r_out2 = a16.alloc(8 * TQ, "out2", high=True)
        emit_attention("a2", kv2_gen, q2_gen, buv_slice, r_out2)
        a16.freer(r_ckv2f)
        a16.freer(r_cq2)
        a16.freer(r_kp)
        a16.freer(r_vp)
        a16.freer(r_qp)
        a16.freer(r_pt)
        a16.freer(wsb.pop("wuk_r"))
        a16.freer(wsb.pop("wuv_r"))
        a16.freer(wsb.pop("wuq_r"))
        a32.freer(r_lbc)
        sch.barrier()

        r_x1bf = a16.alloc(8 * TQ, "x1bf", high=True)
        r_wo2 = load_w("wo2_r", 8, 1024, high=True)
        LDG_B3 = sch.n("ldg")
        r_u2 = a32.alloc(8 * 2048, "u2")
        r_y1c = r_u2
        for m in range(8):
            sch.emit("sync", "dma_start", out=a32.ap(r_u2, 0, 128, m * 2048, 2048),
                     in_=y1_dram[m * 128:(m + 1) * 128, :],
                     waits=[("y1d", sch.n("y1d"))], inc="y1r", incn=16)
        for m in range(8):
            for sg in range(4):
                pb = (m * 4 + sg) % 2
                ps = (P1 if pb == 0 else P2)[:, 0:512]
                for kc in range(8):
                    w = []
                    if kc == 0:
                        w.append(("fin", sch.n("fin")))
                        w.append(("ldg", LDG_B3))
                        w.append(("y1r", 16 * 8))
                        if m * 4 + sg >= 2:
                            w.append(("pje", sch.n("pje") - 1))
                    sch.emit("tensor", "matmul", ps, wtile("wo2_r", kc, m * 128, 128, 1024),
                             a16.ap(r_out2, 0, 128, kc * TQ + sg * 512, 512),
                             start=(kc == 0), stop=(kc == 7), waits=w,
                             inc=("pjm" if kc == 7 else None))
                sch.emit("scalar", "activation", a32.ap(r_ftmp, 0, 128, 0, 512), ps,
                         AF.Identity, bias=pvec("bo2", m),
                         waits=[("pjm", sch.n("pjm")), ("pje", sch.n("pje"))], inc="pjs")
                sch.emit("vector", "tensor_add",
                         a32.ap(r_u2, 0, 128, m * 2048 + sg * 512, 512),
                         a32.ap(r_u2, 0, 128, m * 2048 + sg * 512, 512),
                         a32.ap(r_ftmp, 0, 128, 0, 512),
                         waits=[("pjs", sch.n("pjs"))], inc="pje")

        def x1_write(c, uap):
            for cg in range(4):
                u = a32.ap(r_u2, 0, 128, c * 2048 + cg * 512, 512)
                A_ = a32.ap(r_bcA, 0, 128, cg * 512, 512)
                B_ = a32.ap(r_bcB, 0, 128, cg * 512, 512)
                t = a32.ap(r_sq, 0, 128, 0, 512)
                w = [("bce", sch.n("bce"))] if (c == 0 and cg == 0) else []
                w.append(("x1b", sch.n("x1b")))
                sch.emit("vector", "tensor_mul", t, u, A_, waits=w)
                sch.emit("vector", "tensor_add", t, t, B_, inc="lna")
                sch.emit("scalar", "activation",
                         a16.ap(r_x1bf, 0, 128, c * 2048 + cg * 512, 512), t,
                         AF.Identity, scale=pvec("g1", c), bias=pvec("be1", c),
                         waits=[("lna", sch.n("lna"))], inc="x1b")
            sch.emit("sync", "dma_start", out=u2_dram[c * 128:(c + 1) * 128, :],
                     in_=a32.ap(r_u2, 0, 128, c * 2048, 2048),
                     waits=[("pje", sch.n("pje"))], inc="u2d", incn=16)
            sch.emit("sync", "dma_start", out=x1bf_dram[c * 128:(c + 1) * 128, :],
                     in_=a16.ap(r_x1bf, 0, 128, c * 2048, 2048),
                     waits=[("x1b", sch.n("x1b"))], inc="x1d", incn=16)

        ln_block(r_u2, x1_write, None, None, r_rows, rslot_persist=True,
                 wait_sem=("pje", sch.n("pje")))
        a16.freer(r_out2)
        a16.freer(wsb.pop("wo2_r"))
        sch.barrier()

        r_wg = load_w("wg_r", 8, 8)
        r_wn = load_w("wn_r", 8, 8)
        r_bg = a32.alloc(8, "bg8", high=True)
        r_bn = a32.alloc(8, "bn8", high=True)
        r_noise = a32.alloc(NQC * 8, "noise")
        r_b2sb = a32.alloc(1024, "b2sb", high=True)
        r_gT = a32.alloc(2048, "gT", high=True)
        DBG["gT"] = r_gT
        r_oneh = a32.alloc(1024, "onehot", high=True)
        r_ident = a32.alloc(128, "ident")
        r_gtk = a32.alloc(16 * 8, "g_tokmaj")
        sch.emit("sync", "dma_start", out=a32.ap(r_bg, 0, 1), in_=din["bg8"], inc="ld", incn=16)
        sch.emit("sync", "dma_start", out=a32.ap(r_bn, 0, 1), in_=din["bn8"], inc="ld", incn=16)
        sch.emit("sync", "dma_start", out=a32.ap(r_noise), in_=din["noise_tm"], inc="ld", incn=16)
        sch.emit("sync", "dma_start", out=a32.ap(r_b2sb, 0, 8), in_=din["b2"], inc="ld", incn=16)
        sch.emit("gpsimd", "iota", a32.ap(r_oneh, 0, 8), [[1, 8], [0, 128]],
                 channel_multiplier=-1, allow_small_or_imprecise_dtypes=True, inc="msk")
        sch.emit("gpsimd", "iota", a32.ap(r_ident), [[1, 128]],
                 channel_multiplier=-1, allow_small_or_imprecise_dtypes=True, inc="msk")
        sch.wait("vector", ("msk", sch.n("msk")))
        sch.emit("vector", "tensor_scalar", a32.ap(r_oneh, 0, 8), a32.ap(r_oneh, 0, 8),
                 0.0, None, ALU.is_equal)
        sch.emit("vector", "tensor_scalar", a32.ap(r_ident), a32.ap(r_ident),
                 0.0, None, ALU.is_equal, inc="msk")
        LDGT = sch.n("ld")
        MSKG = sch.n("msk")
        LDG_G = sch.n("ldg")
        for tc in range(16):
            psg = P1[:, 0:8]
            psn = P1[:, 512:520]
            for which, (wnm, bres, ps) in enumerate([("wg_r", r_bg, psg), ("wn_r", r_bn, psn)]):
                for kc in range(8):
                    w = []
                    if kc == 0:
                        w.append(("x1b", sch.n("x1b")))
                        w.append(("ldg", LDG_G))
                        w.append(("ld", LDGT))
                        if tc > 0:
                            w.append(("gte", sch.n("gte")))
                            if os.environ.get("KDBGHW") and which == 0:
                                w.append(("dbghw", sch.n("dbghw")))
                    sch.emit("tensor", "matmul", ps,
                             a16.ap(r_x1bf, 0, 128, kc * 2048 + tc * 128, 128),
                             wtile(wnm, kc, 0, 8, 8),
                             start=(kc == 0), stop=False, waits=w)
                sch.emit("tensor", "matmul", ps, ones_r, a32.ap(bres, 0, 1),
                         start=False, stop=True, inc="gtm")
            hx = a32.ap(r_gtk, 0, 128, 0, 8)
            hn = a32.ap(r_gtk, 0, 128, 8, 8)
            sch.emit("scalar", "activation", hx, psg, AF.Copy,
                     waits=[("gtm", sch.n("gtm"))], inc="gts")
            sch.emit("scalar", "activation", hn, psn, AF.Copy, inc="gts")
            t1 = a32.ap(r_gtk, 0, 128, 16, 8)
            t2 = a32.ap(r_gtk, 0, 128, 24, 8)
            if os.environ.get("KDBGHW"):
                sch.emit("sync", "dma_start",
                         out=att_dbg[:, 10000 + tc * 40: 10000 + tc * 40 + 8],
                         in_=hx, waits=[("gts", sch.n("gts"))], inc="dbg2", incn=16)
            sch.emit("scalar", "activation", t1, hn, AF.Abs,
                     waits=[("gts", sch.n("gts"))], inc="gss")
            sch.emit("scalar", "activation", t1, t1, AF.Exp, scale=-1.0,
                     waits=[("gss", sch.n("gss"))], inc="gss")
            sch.emit("scalar", "activation", t1, t1, AF.Ln, bias=1.0,
                     waits=[("gss", sch.n("gss"))], inc="gts")
            sch.wait("vector", ("gts", sch.n("gts")))
            if os.environ.get("KDBGHW"):
                sch.emit("sync", "dma_start",
                         out=att_dbg[:, 10000 + tc * 40 + 8: 10000 + tc * 40 + 16],
                         in_=t1, waits=[("gts", sch.n("gts"))], inc="dbg2", incn=16)
                sch.wait("vector", ("dbg2", sch.n("dbg2")))
            nz = a32.ap(r_noise, 0, 128, tc * 8, 8)
            def vstep0(meth, *aa, **kk):
                sch.emit("vector", meth, *aa, waits=[("gtv", sch.n("gtv"))], inc="gtv", **kk)
            vstep0("tensor_scalar_max", t2, hn, 0.0)
            vstep0("tensor_add", t1, t1, t2)
            vstep0("tensor_mul", t2, nz, t1)
            vstep0("tensor_add", hx, hx, t2)
            m1 = a32.ap(r_gtk, 0, 128, 32, 1)
            m2 = a32.ap(r_gtk, 0, 128, 33, 1)
            d12 = a32.ap(r_gtk, 0, 128, 34, 1)
            w1 = a32.ap(r_gtk, 0, 128, 35, 1)
            w2 = a32.ap(r_gtk, 0, 128, 36, 1)
            e1 = a32.ap(r_gtk, 0, 128, 40, 8)
            e2 = a32.ap(r_gtk, 0, 128, 48, 8)
            g8 = a32.ap(r_gtk, 0, 128, 56, 8)
            def vstep(meth, *aa, **kk):
                sch.emit("vector", meth, *aa, waits=[("gtv", sch.n("gtv"))], inc="gtv", **kk)
            vstep("reduce_max", m1, hx, axis=AXX)
            vstep("tensor_scalar", e1, hx, m1, None, ALU.is_ge)
            vstep("scalar_tensor_tensor", e2, e1, NEGM, hx, ALU.mult, ALU.add)
            vstep("reduce_max", m2, e2, axis=AXX)
            vstep("tensor_sub", d12, m2, m1)
            sch.emit("scalar", "activation", w1, d12, AF.Sigmoid, scale=-1.0,
                     waits=[("gtv", sch.n("gtv"))], inc="gts")
            sch.wait("vector", ("gts", sch.n("gts")))
            vstep("tensor_scalar", w2, w1, -1.0, 1.0, ALU.mult, ALU.add)
            vstep("tensor_scalar", e2, e2, m2, None, ALU.is_ge)
            vstep("tensor_scalar", g8, e1, w1, None, ALU.mult)
            vstep("scalar_tensor_tensor", g8, e2, w2, g8, ALU.mult, ALU.add)
            if os.environ.get("KDBGHW"):
                sch.emit("sync", "dma_start",
                         out=att_dbg[:, 2048 + tc * 8: 2048 + tc * 8 + 8],
                         in_=hx, waits=[("gtv", sch.n("gtv"))],
                         inc="dbghw", incn=16)
                sch.emit("sync", "dma_start",
                         out=att_dbg[:, 4096 + tc * 24: 4096 + tc * 24 + 8],
                         in_=nz, inc="dbghw", incn=16)
                sch.emit("sync", "dma_start",
                         out=att_dbg[:, 4096 + tc * 24 + 8: 4096 + tc * 24 + 16],
                         in_=t1, inc="dbghw", incn=16)
                sch.emit("sync", "dma_start",
                         out=att_dbg[:, 4096 + tc * 24 + 16: 4096 + tc * 24 + 24],
                         in_=hn, inc="dbghw", incn=16)
                sch.emit("sync", "dma_start",
                         out=att_dbg[:, 8192 + tc * 8: 8192 + tc * 8 + 8],
                         in_=g8, inc="dbghw", incn=16)
                sch.emit("sync", "dma_start",
                         out=att_dbg[:, 10000 + tc * 40 + 16: 10000 + tc * 40 + 24],
                         in_=t2, inc="dbghw", incn=16)
            sch.wait("tensor", ("gtv", sch.n("gtv")), ("msk", MSKG))
            sch.emit("tensor", "transpose", P2[0:8, 0:128], g8, a32.ap(r_ident),
                     inc="gtm")
            sch.emit("scalar", "activation", a32.ap(r_gT, 0, 8, tc * 128, 128),
                     P2[0:8, 0:128], AF.Copy,
                     waits=[("gtm", sch.n("gtm"))], inc="gte")
        if os.environ.get("KDBGHW"):
            sch.emit("sync", "dma_start", out=att_dbg[0:8, 0:2048], in_=a32.ap(r_gT, 0, 8),
                     waits=[("gte", sch.n("gte"))], inc="dbghw", incn=16)
            sch.wait("sync", ("dbghw", sch.n("dbghw")))
        a32.freer(r_u2)
        a32.freer(r_noise)
        a16.freer(r_x1bf)
        a16.freer(wsb.pop("wg_r"))
        a16.freer(wsb.pop("wn_r"))
        a32.freer(r_mh0)
        a32.freer(r_mh1)
        a32.freer(r_ident)
        a32.freer(r_gtk)
        a32.freer(r_bcA)
        a32.freer(r_bcB)
        sch.barrier()
        r_bcA = a32.alloc(1024, "bcA", high=True)
        r_bcB = a32.alloc(1024, "bcB", high=True)

        TT = 1024
        r_hsb = a16.alloc(MC * TT, "hsb")
        r_acc = a32.alloc(8 * TT, "acc")
        r_w1b = a16.alloc(2 * 2048, "w1buf")
        r_w2b = a16.alloc(2 * MC * 128, "w2buf")
        r_Gbc = a32.alloc(TT, "Gbc")
        r_b1 = a32.alloc(E * MC, "b1p")
        r_x1t = a16.alloc(8 * TT, "x1tile", high=True)
        sch.emit("sync", "dma_start", out=a32.ap(r_b1), in_=din["b1p"], inc="ld", incn=16)
        LDB1 = sch.n("ld")
        GTE_ALL = sch.n("gte")

        w1_dma = 0
        w2_dma = 0
        n_out_round = 0
        hr_glob = 0
        for tile in range(2):
            for kc in range(8):
                sch.emit("sync", "dma_start", out=a16.ap(r_x1t, 0, 128, kc * TT, TT),
                         in_=x1bf_dram[kc * 128:(kc + 1) * 128, tile * TT:(tile + 1) * TT],
                         waits=([("x1d", 16 * 8)] if kc == 0 else []) +
                               ([("acs", sch.n("acs"))] if tile > 0 and kc == 0 else []),
                         inc="x1t", incn=16)
            X1T = sch.n("x1t")
            for dc in range(8):
                buf = dc % 2
                po = (P3 if buf == 0 else P4)
                for seg in range(2):
                    w = []
                    if dc == 0 and seg == 0:
                        w += [("ld", LDB1), ("gte", GTE_ALL), ("fin", sch.n("fin"))]
                        if tile > 0:
                            w.append(("acs", sch.n("acs")))
                    if dc >= 2 and seg == 0:
                        w.append(("oev", sch.n("oev") - 1))
                    sch.emit("tensor", "matmul", po[:, seg * 512:(seg + 1) * 512],
                             a32.ap(r_b2sb, 0, 8, dc * 128, 128),
                             a32.ap(r_gT, 0, 8, tile * TT + seg * 512, 512),
                             start=True, stop=True, waits=w,
                             inc=("omm" if seg == 1 else None))
                n_out_round += 1
                sch.emit("vector", "tensor_copy", a32.ap(r_acc, 0, 128, dc * TT, TT),
                         po[:, :], waits=[("omm", sch.n("omm"))], inc="oev")
            for e in range(E):
                for seg in range(2):
                    gw = [("oev", sch.n("oev"))] if seg == 0 else []
                    sch.emit("tensor", "matmul", P3[:, seg * 512:(seg + 1) * 512],
                             a32.ap(r_oneh, 0, 8, e * 128, 128),
                             a32.ap(r_gT, 0, 8, tile * TT + seg * 512, 512),
                             start=True, stop=True, waits=gw,
                             inc=("gbm" if seg == 1 else None))
                sch.emit("scalar", "activation", a32.ap(r_Gbc), P3[:, :], AF.Copy,
                         waits=[("gbm", sch.n("gbm"))], inc="gbe")
                for mg in range(MC // 2):
                    sbi = w1_dma % 2
                    dst = a16.ap(r_w1b, 0, 128, sbi * 2048, 2048)
                    waits = [("hmm", 2 * w1_dma - 2)] if w1_dma >= 2 else []
                    w1sem = "wl1a" if w1_dma % 2 == 0 else "wl1b"
                    sch.emit("gpsimd", "dma_start", out=dst,
                             in_=din["w1r"][e, mg].rearrange("p kc m2 mi -> p (kc m2 mi)"),
                             waits=waits, inc=w1sem, incn=16)
                    w1_dma += 1
                    for m2 in range(2):
                        m = 2 * mg + m2
                        hb_ = m % 2
                        hp_ = (P1 if hb_ == 0 else P2)
                        for kc in range(KC):
                            lhsT = a16.ap(r_w1b, 0, 128, sbi * 2048 + kc * 256 + m2 * 128, 128)
                            for seg in range(2):
                                w = []
                                if kc == 0 and seg == 0:
                                    w.append((("wl1a" if (w1_dma - 1) % 2 == 0 else "wl1b"),
                                              16 * (((w1_dma - 1) // 2) + 1)))
                                    if e == 0 and m == 0:
                                        w.append(("x1t", X1T))
                                        if tile == 0:
                                            w.append(("gte", GTE_ALL))
                                    if hr_glob >= 2:
                                        w.append(("hev", hr_glob - 1))
                                sch.emit("tensor", "matmul",
                                         hp_[:, seg * 512:(seg + 1) * 512], lhsT,
                                         a16.ap(r_x1t, 0, 128, kc * TT + seg * 512, 512),
                                         start=(kc == 0), stop=(kc == KC - 1), waits=w,
                                         inc=("hmm" if (kc == KC - 1 and seg == 1) else None))
                        hr_glob += 1
                        sch.emit("scalar", "activation", a16.ap(r_hsb, 0, 128, m * TT, TT),
                                 hp_[:, :], AF.Relu,
                                 bias=a32.ap(r_b1, 0, 128, e * MC + m, 1),
                                 waits=[("hmm", sch.n("hmm"))], inc="hev")
                for dc in range(8):
                    j = w2_dma
                    sbi = j % 2
                    dst = a16.ap(r_w2b, 0, 128, sbi * MC * 128, MC * 128)
                    waits = [("omm", sch.n("omm") - 1)] if j >= 2 else []
                    w2sem = "wl2a" if j % 2 == 0 else "wl2b"
                    sch.emit("gpsimd", "dma_start", out=dst,
                             in_=din["w2r"][e, dc].rearrange("p mc di -> p (mc di)"),
                             waits=waits, inc=w2sem, incn=16)
                    w2_dma += 1
                    ob = n_out_round % 2
                    po = (P3 if ob == 0 else P4)
                    for mc in range(MC):
                        lhsT = a16.ap(r_w2b, 0, 128, sbi * MC * 128 + mc * 128, 128)
                        for seg in range(2):
                            w = []
                            if mc == 0 and seg == 0:
                                w.append((("wl2a" if (w2_dma - 1) % 2 == 0 else "wl2b"),
                                          16 * (((w2_dma - 1) // 2) + 1)))
                                w.append(("hev", sch.n("hev")))
                                if n_out_round >= 2:
                                    w.append(("oev", sch.n("oev") - 1))
                                if dc == 0:
                                    w.append(("gbe", sch.n("gbe")))
                            sch.emit("tensor", "matmul", po[:, seg * 512:(seg + 1) * 512],
                                     lhsT, a16.ap(r_hsb, 0, 128, mc * TT + seg * 512, 512),
                                     start=(mc == 0), stop=(mc == MC - 1), waits=w,
                                     inc=("omm" if (mc == MC - 1 and seg == 1) else None))
                    n_out_round += 1
                    sch.emit("vector", "tensor_tensor", a32.ap(r_ftmp, 0, 128, 0, TT),
                             po[:, :], a32.ap(r_Gbc), ALU.mult,
                             waits=[("omm", sch.n("omm")), ("gbe", sch.n("gbe"))])
                    sch.emit("vector", "tensor_add", a32.ap(r_acc, 0, 128, dc * TT, TT),
                             a32.ap(r_acc, 0, 128, dc * TT, TT),
                             a32.ap(r_ftmp, 0, 128, 0, TT), inc="oev")

            if tile == 0:
                r_fu = a32.alloc(TT, "finale_u")
            sch.emit("sync", "dma_start", out=ROW(r_rows, 2, 0, TT),
                     in_=rows_dram[0:1, tile * TT:(tile + 1) * TT],
                     waits=[("rwd", 32), ("fcv", sch.n("fcv"))], inc="prl", incn=16)
            sch.emit("sync", "dma_start", out=ROW(r_rows, 2, TT, 2048 - TT)
                     if False else ROW(r_rows, 0, 1024, TT),
                     in_=rows_dram[1:2, tile * TT:(tile + 1) * TT],
                     inc="prl", incn=16)
            for sg in range(2):
                w = [("oev", sch.n("oev")), ("prl", sch.n("prl"))] if sg == 0 else []
                sch.emit("tensor", "matmul", P1[:, sg * 512:(sg + 1) * 512], ones_r,
                         ROW(r_rows, 2, sg * 512, 512),
                         start=True, stop=True, waits=w)
                sch.emit("tensor", "matmul", P2[:, sg * 512:(sg + 1) * 512], ones_r,
                         ROW(r_rows, 0, 1024 + sg * 512, 512),
                         start=True, stop=True, inc="bcm")
            sch.emit("scalar", "activation", a32.ap(r_bcA, 0, 128, 0, TT), P1[:, 0:TT], AF.Copy,
                     waits=[("bcm", sch.n("bcm"))])
            sch.emit("scalar", "activation", a32.ap(r_bcB, 0, 128, 0, TT), P2[:, 0:TT], AF.Copy,
                     inc="bce")
            BCE2 = sch.n("bce")
            for c in range(8):
                fs = "fua"
                u2c = a32.ap(r_fu, 0, 128, 0, TT)
                sch.emit("sync", "dma_start", out=u2c,
                         in_=u2_dram[c * 128:(c + 1) * 128, tile * TT:(tile + 1) * TT],
                         waits=[("u2d", 16 * 8)] + ([("fcv", sch.n("fcv"))] if c >= 1 or tile > 0 else []),
                         inc=fs, incn=16)
                accc = a32.ap(r_acc, 0, 128, c * TT, TT)
                t = a32.ap(r_ftmp, 0, 128, 0, TT)
                sch.wait("vector", (fs, 16 * sch.n(fs) // 16),
                         ("oev", sch.n("oev")), ("bce", BCE2))
                sch.emit("vector", "tensor_mul", t, u2c, a32.ap(r_bcA, 0, 128, 0, TT))
                sch.emit("vector", "tensor_add", t, t, a32.ap(r_bcB, 0, 128, 0, TT), inc="fcv")
                sch.emit("scalar", "activation", u2c, t, AF.Identity,
                         scale=pvec("g1", c), bias=pvec("be1", c),
                         waits=[("fcv", sch.n("fcv"))], inc="fcs")
                sch.wait("vector", ("fcs", sch.n("fcs")))
                sch.emit("vector", "tensor_add", accc, accc, u2c, inc="fcv")
            fin_stats_emitted = []

            def fin_stats(r_src, ncols):
                for cg in range(ncols // 512):
                    for c in range(8):
                        w = []
                        if c == 0:
                            w.append(("fcv", sch.n("fcv")))
                            if cg > 0:
                                w.append(("lse", sch.n("lse")))
                        sch.emit("tensor", "matmul", P1[0:1, 0:512], ones_c,
                                 a32.ap(r_src, 0, 128, c * ncols + cg * 512, 512),
                                 start=(c == 0), stop=(c == 7), waits=w,
                                 inc=("lsm" if c == 7 else None))
                    for c in range(8):
                        sch.emit("scalar", "activation", a32.ap(r_sq, 0, 128, 0, 512),
                                 a32.ap(r_src, 0, 128, c * ncols + cg * 512, 512), AF.Square,
                                 waits=([("lsm", sch.n("lsm"))] if c == 0 else
                                        [("sqe", sch.n("sqe"))]), inc="sqp")
                        sch.emit("tensor", "matmul", P2[0:1, 0:512], ones_c,
                                 a32.ap(r_sq, 0, 128, 0, 512),
                                 start=(c == 0), stop=(c == 7),
                                 waits=[("sqp", sch.n("sqp"))],
                                 inc=("lsm" if c == 7 else "sqe"))
                    sch.emit("vector", "tensor_copy", ROW(r_rows, 0, cg * 512, 512),
                             P1[0:1, 0:512], waits=[("lsm", sch.n("lsm"))], inc="lse")
                    sch.emit("vector", "tensor_copy", ROW(r_rows, 1, cg * 512, 512),
                             P2[0:1, 0:512], inc="lse")
                Arow = ROW(r_rows, 0, 0, ncols)
                Brow = ROW(r_rows, 1, 0, ncols)
                Crow = ROW(r_rows, 2, 0, ncols)
                sch.wait("vector", ("lse", sch.n("lse")))
                sch.emit("vector", "tensor_scalar_mul", Arow, Arow, 1.0 / 1024)
                sch.emit("vector", "tensor_scalar_mul", Brow, Brow, 1.0 / 1024)
                sch.emit("vector", "tensor_mul", Crow, Arow, Arow)
                sch.emit("vector", "tensor_sub", Brow, Brow, Crow)
                sch.emit("vector", "tensor_scalar_add", Brow, Brow, EPS, inc="lse")
                sch.emit("scalar", "activation", Brow, Brow, AF.Sqrt,
                         waits=[("lse", sch.n("lse"))], inc="lse")
                sch.wait("vector", ("lse", sch.n("lse")))
                sch.emit("vector", "reciprocal", Brow, Brow)
                sch.emit("vector", "scalar_tensor_tensor", Crow, Arow, -1.0, Brow,
                         ALU.mult, ALU.mult, inc="lse")
                sch.wait("tensor", ("lse", sch.n("lse")))
                for sg in range(ncols // 512):
                    sch.emit("tensor", "matmul", P1[:, 0:512], ones_r,
                             Brow[:, sg * 512:(sg + 1) * 512], start=True, stop=True,
                             waits=[("bce", sch.n("bce"))] if sg > 0 else [])
                    sch.emit("tensor", "matmul", P2[:, 0:512], ones_r,
                             Crow[:, sg * 512:(sg + 1) * 512], start=True, stop=True,
                             inc="bcm")
                    sch.emit("scalar", "activation", a32.ap(r_bcA, 0, 128, sg * 512, 512),
                             P1[:, 0:512], AF.Copy, waits=[("bcm", sch.n("bcm"))])
                    sch.emit("scalar", "activation", a32.ap(r_bcB, 0, 128, sg * 512, 512),
                             P2[:, 0:512], AF.Copy, inc="bce")

            fin_stats(r_acc, TT)
            for c in range(8):
                fs = "fua"
                accc = a32.ap(r_acc, 0, 128, c * TT, TT)
                t = a32.ap(r_ftmp, 0, 128, 0, TT)
                y1c = a32.ap(r_fu, 0, 128, 0, TT)
                sch.emit("sync", "dma_start", out=y1c,
                         in_=y1_dram[c * 128:(c + 1) * 128, tile * TT:(tile + 1) * TT],
                         waits=[("fcv", sch.n("fcv"))], inc=fs, incn=16)
                w = [("bce", sch.n("bce"))] if c == 0 else []
                sch.emit("vector", "tensor_mul", t, accc, a32.ap(r_bcA, 0, 128, 0, TT),
                         waits=w)
                sch.emit("vector", "tensor_add", t, t, a32.ap(r_bcB, 0, 128, 0, TT),
                         inc="fcv")
                sch.emit("scalar", "activation", accc, t, AF.Identity,
                         scale=pvec("g2", c), bias=pvec("be2", c),
                         waits=[("fcv", sch.n("fcv"))], inc="fcs")
                sch.wait("vector", ("fcs", sch.n("fcs")), (fs, 16 * (sch.n(fs) // 16)))
                sch.emit("vector", "tensor_add", accc, accc, y1c, inc="fcv")
            fin_stats(r_acc, TT)
            for c in range(8):
                accc = a32.ap(r_acc, 0, 128, c * TT, TT)
                t = a32.ap(r_ftmp, 0, 128, 0, TT)
                w = [("bce", sch.n("bce"))] if c == 0 else []
                sch.emit("vector", "tensor_mul", t, accc, a32.ap(r_bcA, 0, 128, 0, TT),
                         waits=w)
                sch.emit("vector", "tensor_add", accc, t, a32.ap(r_bcB, 0, 128, 0, TT),
                         inc="fcv")
                sch.emit("sync", "dma_start",
                         out=out_d[c * 128:(c + 1) * 128, tile * TT:(tile + 1) * TT],
                         in_=accc, waits=[("fcv", sch.n("fcv"))], inc="acs", incn=16)
        sch.wait("sync", ("acs", sch.n("acs")))

        sch.replay_all(block)

    nc._pool_live = dict(pool.live)
    nc._dbg = dict(DBG)
    nc._pool = pool
    n = sum(len(v) for v in sch.prog.values())
    print(f"kernel instructions: {n} "
          f"(tensor={len(sch.prog['tensor'])}, scalar={len(sch.prog['scalar'])}, "
          f"vector={len(sch.prog['vector'])}, gpsimd={len(sch.prog['gpsimd'])}, "
          f"sync={len(sch.prog['sync'])}) sems={len(sch.sems)} "
          f"hwm={pool.hwm}B")
    return nc


_NC_CACHE = {}


def _host_stage(inputs):
    import jax
    x = np.asarray(inputs["L0_attn_x"])
    with jax.default_device(jax.devices("cpu")[0]):
        noise = np.asarray(jax.random.normal(jax.random.key(42, impl="threefry2x32"),
                                             (B, S, E), dtype="float32"))

    dr = 128
    inv = 1.0 / (10000.0 ** (np.arange(0, dr, 2, dtype=np.float64) / dr))
    tpos = np.arange(S, dtype=np.float64)
    fr = tpos[:, None] * inv[None, :]
    emb = np.concatenate([fr, fr], axis=-1)
    cosT = np.cos(emb).astype(np.float32).T
    sinT = np.sin(emb).astype(np.float32).T

    par_order = np.concatenate([np.arange(0, 32, 2), np.arange(1, 32, 2)])
    perm = (par_order[:, None] * 128 + np.arange(128)[None, :]).reshape(-1)

    def pcols(v):
        v = np.asarray(v, np.float32).reshape(-1)
        ncol = (len(v) + 127) // 128
        return v.reshape(ncol, 128).T
    pv = np.zeros((128, PV_NCOL), np.float32)
    pmap = {"bq": "L0_attn_bq", "bqc": "L0_attn_bqc", "bqr": "L0_attn_bqr",
            "bkv": "L0_attn_bkv", "bkr": "L0_attn_bkr", "bkc": "L0_attn_bkc",
            "bv": "L0_attn_bv", "bo": "L0_attn_bo", "bdkv": "L0_ffn_bdkv",
            "buk": "L0_ffn_buk", "buv": "L0_ffn_buv", "bdq": "L0_ffn_bdq",
            "buq": "L0_ffn_buq", "bo2": "L0_ffn_bo", "g1": "L0_ffn_gamma1",
            "be1": "L0_ffn_beta1", "g2": "L0_ffn_gamma2", "be2": "L0_ffn_beta2"}
    for nme, src in pmap.items():
        cc = pcols(inputs[src])
        pv[:, PV_COL[nme]:PV_COL[nme] + cc.shape[1]] = cc

    wr = {}
    for nme, src in [("wq_r", "L0_attn_wq"), ("wqc_r", "L0_attn_wqc"),
                     ("wqr_r", "L0_attn_wqr"), ("wkv_r", "L0_attn_wkv"),
                     ("wkr_r", "L0_attn_wkr"), ("wkc_r", "L0_attn_wkc"),
                     ("wv_r", "L0_attn_wv"), ("wo_r", "L0_attn_wo"),
                     ("wdkv_r", "L0_ffn_Wdkv"), ("wdq_r", "L0_ffn_Wdq"),
                     ("wuk_r", "L0_ffn_Wuk"), ("wuv_r", "L0_ffn_Wuv"),
                     ("wuq_r", "L0_ffn_Wuq"), ("wo2_r", "L0_ffn_Wo"),
                     ("wg_r", "L0_ffn_Wg"), ("wn_r", "L0_ffn_Wn")]:
        wr[nme] = relayout_lhsT(np.asarray(inputs[src], np.float32))
    w1r = relayout_w1(np.asarray(inputs["L0_ffn_W1"], np.float32))
    w2r = relayout_w2(np.asarray(inputs["L0_ffn_W2"], np.float32))
    b1p = np.ascontiguousarray(
        np.asarray(inputs["L0_ffn_b1"], np.float32).reshape(E * MC, 128).T)
    b2v = np.asarray(inputs["L0_ffn_b2"], np.float32)
    bg8 = np.asarray(inputs["L0_ffn_bg"], np.float32).reshape(1, 8)
    bn8 = np.asarray(inputs["L0_ffn_bn"], np.float32).reshape(1, 8)

    tril = (np.tril(np.ones((128, 128), np.float32), -1) * NEGM)
    full = np.full((128, 128), NEGM, np.float32)
    zero = np.zeros((128, 128), np.float32)

    in_maps = []
    for c in range(8):
        b = c // 2
        h = c % 2
        xb = np.ascontiguousarray(x[b].T)
        xT_all = np.ascontiguousarray(xb[:, perm])
        own_chunks = par_order_own = np.arange(h, 32, 2)
        own_tok = (own_chunks[:, None] * 128 + np.arange(128)[None, :]).reshape(-1)
        xT_own = np.ascontiguousarray(xb[:, own_tok])
        sswapT = np.concatenate([sinT[64:128], sinT[0:64]], axis=0)
        cs_q = np.stack([np.ascontiguousarray(cosT[:, own_tok]),
                         np.ascontiguousarray(sswapT[:, own_tok])])
        cs_k = np.stack([np.ascontiguousarray(cosT[:, perm]),
                         np.ascontiguousarray(sswapT[:, perm])])
        if h == 0:
            m0, m1 = tril, full
        else:
            m0, m1 = zero, tril
        ntm = noise[b][own_tok].reshape(NQC, 128, E).transpose(1, 0, 2).reshape(128, NQC * E)
        im = {"xT_own": xT_own, "xT_all": xT_all, "cs_q": cs_q, "cs_k": cs_k,
              "mh0": np.ascontiguousarray(m0), "mh1": np.ascontiguousarray(m1),
              "noise_tm": np.ascontiguousarray(ntm), "pvec": pv, "bg8": bg8, "bn8": bn8,
              "b2": b2v, "b1p": b1p, "w1r": w1r, "w2r": w2r}
        im.update(wr)
        in_maps.append(im)
    return in_maps


def kernel(**inputs):
    in_maps = _host_stage(inputs)
    if "nc" not in _NC_CACHE:
        _NC_CACHE["nc"] = build()
    nc = _NC_CACHE["nc"]
    res = run_bass_kernel_spmd(nc, in_maps, core_ids=list(range(8)))
    y = np.zeros((B, S, D), np.float32)
    for c in range(8):
        b = c // 2
        h = c % 2
        o = res.results[c]["out"]
        own_chunks = np.arange(h, 32, 2)
        for j, g in enumerate(own_chunks):
            y[b, g * 128:(g + 1) * 128, :] = o[:, j * 128:(j + 1) * 128].T
    return y


# version 06 (final)
# speedup vs baseline: 1.1820x; 1.1820x over previous
import numpy as np
import concourse.bass as bass
import concourse.mybir as mybir
from concourse.bass_utils import run_bass_kernel_spmd
from contextlib import ExitStack

F32 = mybir.dt.float32
BF16 = mybir.dt.bfloat16
AF = mybir.ActivationFunctionType
ALU = mybir.AluOpType
AXX = mybir.AxisListType.X

B, S, D = 4, 4096, 1024
TQ = 2048
NQC = 16
HKC = 16
QT = 1024
NQT = 2
QCPT = 8
E, FF = 8, 4096
MC = FF // 128
KC = D // 128
SHIFT = -12.0
NEGM = -1e30
EPS = 1e-5

PV_ITEMS = [("bq", 256), ("bqc", 896), ("bqr", 128), ("bkv", 256), ("bkr", 128),
            ("bkc", 896), ("bv", 1024), ("bo", 1024), ("bdkv", 512), ("buk", 1024),
            ("buv", 1024), ("bdq", 512), ("buq", 1024), ("bo2", 1024),
            ("g1", 1024), ("be1", 1024), ("g2", 1024), ("be2", 1024)]
PV_COL = {}
_c = 0
for _n, _l in PV_ITEMS:
    PV_COL[_n] = _c
    _c += (_l + 127) // 128
PV_NCOL = _c


BF16_ONES_PAIR = 1.0019378662109375


def relayout_lhsT(W):
    K, M = W.shape
    kc = K // 128
    return np.ascontiguousarray(W.reshape(kc, 128, M).transpose(1, 0, 2)).reshape(128, kc * M)


def relayout_w1(W1):
    W = W1.reshape(E, KC, 128, MC // 2, 2, 128)
    return np.ascontiguousarray(W.transpose(0, 3, 2, 1, 4, 5))


def relayout_w2(W2):
    W = W2.reshape(E, MC, 128, KC, 128)
    return np.ascontiguousarray(W.transpose(0, 3, 2, 1, 4))


class BytePool:
    def __init__(self, nbytes):
        self.free = [(0, nbytes)]
        self.hwm = 0
        self.live = {}

    def alloc(self, nb, tag="", high=False):
        nb = (nb + 3) & ~3
        rng = list(enumerate(self.free))
        if high:
            for i, (a, b) in reversed(rng):
                if b - a >= nb:
                    self.free[i] = (a, b - nb)
                    if self.free[i][0] == self.free[i][1]:
                        self.free.pop(i)
                    self._chk(b - nb, nb, tag)
                    self.live[b - nb] = (nb, tag)
                    return b - nb
        else:
            for i, (a, b) in rng:
                if b - a >= nb:
                    self.free[i] = (a + nb, b)
                    if self.free[i][0] == self.free[i][1]:
                        self.free.pop(i)
                    self.hwm = max(self.hwm, a + nb)
                    self.live[a] = (nb, tag)
                    return a
        raise MemoryError(f"arena OOM for {tag}: need {nb}B, free={self.free}, "
                          f"live={sorted((v[1], v[0]) for v in self.live.values())}")

    def _chk(self, a, nb, tag):
        for o, (n2, t2) in self.live.items():
            if a < o + n2 and o < a + nb:
                raise RuntimeError(f"arena overlap: {tag}@[{a},{a+nb}) with {t2}@[{o},{o+n2})")

    def freeb(self, a, nb):
        nb = (nb + 3) & ~3
        assert a in self.live, f"double free at {a}"
        self.live.pop(a, None)
        self.free.append((a, a + nb))
        self.free.sort()
        merged = []
        for seg in self.free:
            if merged and merged[-1][1] >= seg[0]:
                merged[-1] = (merged[-1][0], max(merged[-1][1], seg[1]))
            else:
                merged.append(list(seg))
        self.free = [tuple(x) for x in merged]


class Arena:
    def __init__(self, handle, pool, elsz):
        self.h = handle
        self.pool = pool
        self.elsz = elsz

    def alloc(self, ncols, tag="", high=False):
        off = self.pool.alloc(ncols * self.elsz, tag, high=high)
        return (off, ncols, self.elsz)

    def freer(self, r):
        self.pool.freeb(r[0], r[1] * r[2])

    def ap(self, r, p0=0, p1=128, c0=0, ncols=None):
        if ncols is None:
            ncols = r[1] - c0
        boff = r[0] + c0 * self.elsz
        if self.elsz == 2:
            return self.h[p0:p1, boff // 2: boff // 2 + ncols]
        ap = self.h[p0:p1, boff // 2: boff // 2 + 2 * ncols]
        return ap.bitcast(F32)


class Sched:
    def __init__(self, nc, es):
        self.nc = nc
        self.es = es
        self.prog = {k: [] for k in ("tensor", "vector", "scalar", "gpsimd", "sync")}
        self.sems = {}
        self.cnt = {}

    def sem(self, name):
        if name not in self.sems:
            self.sems[name] = self.es.enter_context(self.nc.semaphore(name))
            self.cnt[name] = 0
        return name

    def n(self, name):
        return self.cnt.get(name, 0)

    def emit(self, eng, meth, *a, waits=(), inc=None, incn=1, **kw):
        for s, v in waits:
            self.sem(s)
        if inc:
            self.sem(inc)
            self.cnt[inc] += incn
            if incn == 16:
                if not hasattr(self, "dmasems"):
                    self.dmasems = set()
                self.dmasems.add(inc)
        self.prog[eng].append((meth, a, kw, list(waits), inc, incn))
        return self.cnt[inc] if inc else None

    def barrier(self):
        self.nbar = getattr(self, "nbar", 0) + 1
        dmaw = [(s_, self.cnt[s_]) for s_ in sorted(getattr(self, "dmasems", set()))]
        self.emit("sync", "nop", waits=dmaw, inc="bar")
        for eng in ("tensor", "vector", "scalar", "gpsimd"):
            self.emit(eng, "nop", inc="bar")
        for eng in ("tensor", "vector", "scalar", "gpsimd", "sync"):
            self.wait(eng, ("bar", 5 * self.nbar))

    def wait(self, eng, *waits):
        self.emit(eng, "wait_only", waits=list(waits))

    def replay_all(self, block):
        def mk(key):
            def run(eng):
                for i, (meth, a, kw, waits, inc, incn) in enumerate(self.prog[key]):
                    for s, v in waits:
                        eng.wait_ge(self.sems[s], v)
                    if meth == "wait_only":
                        continue
                    if meth == "__dbg__":
                        cb, ap_ = a
                        eng.dbg_aps(cb, ap_)
                        continue
                    inst = getattr(eng, meth)(*a, **kw)
                    inst.annotate(f"{key}#{i}")
                    if inc:
                        inst.then_inc(self.sems[inc], incn)
            return run
        block.tensor(mk("tensor"))
        block.vector(mk("vector"))
        block.scalar(mk("scalar"))
        block.gpsimd(mk("gpsimd"))
        block.sync(mk("sync"))


def build():
    import os
    nc = bass.Bass("TRN2", target_bir_lowering=False, debug=bool(os.environ.get("KDBG")))

    din = {}
    def D_(name, shape):
        din[name] = nc.dram_tensor(name, list(shape), F32, kind="ExternalInput").ap()
    D_("xT_own", (D, TQ))
    D_("xT_all", (D, S))
    D_("cs_q", (2, 128, TQ))
    D_("cs_k", (2, 128, S))
    D_("mh0", (128, 128))
    D_("mh1", (128, 128))
    D_("noise_tm", (128, NQC * 8))
    D_("pvec", (128, PV_NCOL))
    D_("bg8", (1, 8))
    D_("bn8", (1, 8))
    D_("b2", (E, D))
    D_("b1p", (128, E * MC))
    for w, k, m in [("wq", 8, 256), ("wqc", 2, 896), ("wqr", 8, 128), ("wkv", 8, 256),
                    ("wkr", 8, 128), ("wkc", 2, 896), ("wv", 2, 1024), ("wo", 8, 1024),
                    ("wdkv", 8, 512), ("wdq", 8, 512), ("wuk", 4, 1024), ("wuv", 4, 1024),
                    ("wuq", 4, 1024), ("wo2", 8, 1024), ("wg", 8, 8), ("wn", 8, 8)]:
        D_(w + "_r", (128, k * m))
    din["w1r"] = nc.dram_tensor("w1r", [E, MC // 2, 128, KC, 2, 128], F32, kind="ExternalInput").ap()
    din["w2r"] = nc.dram_tensor("w2r", [E, KC, 128, MC, 128], F32, kind="ExternalInput").ap()
    out_d = nc.dram_tensor("out", [D, TQ], F32, kind="ExternalOutput").ap()
    dbg_kind = dict(kind="ExternalOutput") if os.environ.get("KDBGHW") else {}
    cc_in = nc.dram_tensor("cc_in", [512, TQ], BF16)
    cc_out = nc.dram_tensor("cc_out", [1024, TQ], BF16)
    y1_dram = nc.dram_tensor("y1_dram", [D, TQ], F32, **dbg_kind)
    u1_dbg = nc.dram_tensor("u1_dbg", [D, TQ], F32)
    o1_dbg = nc.dram_tensor("o1_dbg", [D, TQ], F32, **dbg_kind)
    att_dbg = nc.dram_tensor("att_dbg", [128, 16384], F32, **dbg_kind)
    u2_dram = nc.dram_tensor("u2_dram", [D, TQ], F32, **dbg_kind)
    rows_dram = nc.dram_tensor("rows_dram", [2, TQ], F32)
    x1bf_dram = nc.dram_tensor("x1bf_dram", [D, TQ], BF16, **dbg_kind)

    es = ExitStack()
    sch = Sched(nc, es)
    with es:
        ARh = es.enter_context(nc.sbuf_tensor([128, 106000], BF16))
        pool = BytePool(106000 * 2)
        a32 = Arena(ARh, pool, 4)
        a16 = Arena(ARh, pool, 2)
        P1 = es.enter_context(nc.psum_tensor([128, 1024], F32))
        P2 = es.enter_context(nc.psum_tensor([128, 1024], F32))
        P3 = es.enter_context(nc.psum_tensor([128, 1024], F32))
        P4 = es.enter_context(nc.psum_tensor([128, 1024], F32))
        block = es.enter_context(nc.Block())

        r_pvec = a32.alloc(PV_NCOL, "pvec")
        r_ones = a32.alloc(128, "ones")
        r_shift = a32.alloc(1, "shift")
        r_mh0 = a32.alloc(128, "mh0")
        r_mh1 = a32.alloc(128, "mh1")
        r_onesbf = a16.alloc(2, "onesbf")
        r_rows = a32.alloc(6144, "rows", high=True)

        pvec = lambda name, c, p0=0, p1=128: a32.ap(r_pvec, p0, p1, PV_COL[name] + c, 1)
        ones_c = a32.ap(r_ones, 0, 128, 0, 1)
        ones_r = a32.ap(r_ones, 0, 1, 0, 128)
        ones_r64 = a32.ap(r_ones, 64, 65, 0, 128)
        ones_r32 = a32.ap(r_ones, 32, 33, 0, 128)
        onesbf_c = a16.ap(r_onesbf, 0, 128, 0, 1)
        shiftv = a32.ap(r_shift, 0, 128, 0, 1)
        mh0 = a32.ap(r_mh0)
        mh1 = a32.ap(r_mh1)
        ROW = lambda r, slot, c0=0, n=2048: a32.ap(r, 0, 1, slot * 2048 + c0, n)

        sch.emit("sync", "dma_start", out=a32.ap(r_pvec), in_=din["pvec"], inc="ld", incn=16)
        sch.emit("sync", "dma_start", out=mh0, in_=din["mh0"], inc="ld", incn=16)
        sch.emit("sync", "dma_start", out=mh1, in_=din["mh1"], inc="ld", incn=16)
        sch.emit("gpsimd", "memset", a32.ap(r_ones), 1.0, inc="msk")
        sch.emit("gpsimd", "memset", shiftv, SHIFT, inc="msk")
        sch.emit("gpsimd", "memset", onesbf_c, 1.0, inc="msk")
        LD0 = sch.n("ld")
        MSK0 = sch.n("msk")

        wsb = {}
        def load_w(name, kc, m, arena_tag=None, high=False):
            r = a16.alloc(kc * m, arena_tag or name, high=high)
            wsb[name] = r
            sch.emit("gpsimd", "dma_start", out=a16.ap(r), in_=din[name],
                     inc="ldg", incn=16)
            return r
        def wtile(name, kci, m0, mw, M):
            r = wsb[name]
            return a16.ap(r, 0, 128, kci * M + m0, mw)

        r_x0 = a32.alloc(8 * 2048, "x0")
        r_x0bf = a16.alloc(8 * 2048, "x0bf")
        for c in range(8):
            sch.emit("sync", "dma_start", out=a32.ap(r_x0, 0, 128, c * 2048, 2048),
                     in_=din["xT_own"][c * 128:(c + 1) * 128, :], inc="ld", incn=16)
            sch.emit("gpsimd", "dma_start", out=a16.ap(r_x0bf, 0, 128, c * 2048, 2048),
                     in_=din["xT_own"][c * 128:(c + 1) * 128, :], inc="ldg", incn=16)
        load_w("wq_r", 8, 256)
        load_w("wqr_r", 8, 128)
        LDG_A = sch.n("ldg")
        LD_A = sch.n("ld")

        r_cq1 = a16.alloc(2 * TQ, "cq1")
        r_qr1 = a16.alloc(TQ, "qr1r")
        r_csq = a32.alloc(2 * TQ, "csq")
        r_krp = a32.alloc(2 * 512, "krp")
        r_rtmp = a32.alloc(2 * 512, "ropetmp")
        for t in range(2):
            sch.emit("sync", "dma_start", out=a32.ap(r_csq, 0, 128, t * TQ, TQ),
                     in_=din["cs_q"][t], inc="ld", incn=16)
        LD_A2 = sch.n("ld")
        for seg in range(4):
            for m in range(2):
                ps = P1[:, m * 512:(m + 1) * 512]
                for kc in range(KC):
                    w = []
                    if kc == 0:
                        w.append(("aev", sch.n("aev")))
                        if seg == 0 and m == 0:
                            w += [("ldg", LDG_A), ("ld", LD_A)]
                    sch.emit("tensor", "matmul", ps, wtile("wq_r", kc, m * 128, 128, 256),
                             a16.ap(r_x0bf, 0, 128, kc * 2048 + seg * 512, 512),
                             start=(kc == 0), stop=(kc == KC - 1), waits=w,
                             inc=("amm" if kc == KC - 1 else None))
                sch.emit("scalar", "activation", a16.ap(r_cq1, 0, 128, m * TQ + seg * 512, 512),
                         ps, AF.Gelu, bias=pvec("bq", m),
                         waits=[("amm", sch.n("amm"))], inc="aev")
            psq = P2[:, (seg % 2) * 512:(seg % 2) * 512 + 512]
            for kc in range(KC):
                w = []
                if kc == 0:
                    w.append(("rev", sch.n("rev")))
                sch.emit("tensor", "matmul", psq, wtile("wqr_r", kc, 0, 128, 128),
                         a16.ap(r_x0bf, 0, 128, kc * 2048 + seg * 512, 512),
                         start=(kc == 0), stop=(kc == KC - 1), waits=w,
                         inc=("amm" if kc == KC - 1 else None))
            sch.emit("scalar", "activation", a32.ap(r_krp, 0, 128, (seg % 2) * 512, 512), psq,
                     AF.Identity, bias=pvec("bqr", 0),
                     waits=[("amm", sch.n("amm")), ("rev", sch.n("rev"))], inc="kpe")
            x1a = a32.ap(r_krp, 0, 64, (seg % 2) * 512, 512)
            x2a = a32.ap(r_krp, 64, 128, (seg % 2) * 512, 512)
            cos_t = a32.ap(r_csq, 0, 64, seg * 512, 512)
            cos_b = a32.ap(r_csq, 64, 128, seg * 512, 512)
            sin_t64 = a32.ap(r_csq, 64, 128, TQ + seg * 512, 512)
            sin_b0 = a32.ap(r_csq, 0, 64, TQ + seg * 512, 512)
            ta = a32.ap(r_rtmp, 0, 64, 0, 512)
            tb = a32.ap(r_rtmp, 0, 64, 512, 512)
            tc = a32.ap(r_rtmp, 64, 128, 0, 512)
            td = a32.ap(r_rtmp, 64, 128, 512, 512)
            qo_t = a16.ap(r_qr1, 0, 64, seg * 512, 512)
            qo_b = a16.ap(r_qr1, 64, 128, seg * 512, 512)
            sch.emit("vector", "tensor_mul", ta, x1a, cos_t,
                     waits=[("kpe", sch.n("kpe")), ("ld", LD_A2)])
            sch.emit("vector", "tensor_mul", tb, x2a, sin_t64)
            sch.emit("vector", "tensor_sub", qo_t, ta, tb)
            sch.emit("vector", "tensor_mul", tc, x2a, cos_b)
            sch.emit("vector", "tensor_mul", td, x1a, sin_b0)
            sch.emit("vector", "tensor_add", qo_b, tc, td, inc="rev")
        a16.freer(r_x0bf)
        a32.freer(r_csq)
        a16.freer(wsb.pop("wq_r"))
        a16.freer(wsb.pop("wqr_r"))
        sch.barrier()
        load_w("wkv_r", 8, 256)
        load_w("wkr_r", 8, 128)
        load_w("wqc_r", 2, 896)
        load_w("wkc_r", 2, 896)
        load_w("wv_r", 2, 1024)
        LDG_A1 = sch.n("ldg")

        REVQ = sch.n("rev")
        r_ckv1 = a16.alloc(2 * S, "ckv1")
        r_kr1 = a16.alloc(S, "kr1r")
        r_xblk = a16.alloc(2 * 8 * 512, "xblk")
        r_csk = a32.alloc(2 * 2 * 512, "cskblk")
        for blk in range(8):
            bb = blk % 2
            cw = [("aev", sch.n("aev"))] if blk >= 2 else []
            for kc in range(KC):
                sch.emit("gpsimd", "dma_start",
                         out=a16.ap(r_xblk, 0, 128, bb * 4096 + kc * 512, 512),
                         in_=din["xT_all"][kc * 128:(kc + 1) * 128, blk * 512:(blk + 1) * 512],
                         waits=(cw if kc == 0 else []),
                         inc=("xba" if bb == 0 else "xbb"), incn=16)
            for t in range(2):
                sch.emit("sync", "dma_start",
                         out=a32.ap(r_csk, 0, 128, bb * 1024 + t * 512, 512),
                         in_=din["cs_k"][t, :, blk * 512:(blk + 1) * 512],
                         waits=(cw if t == 0 else []),
                         inc=("csa" if bb == 0 else "csb"), incn=16)
            xw = ("xba" if bb == 0 else "xbb"), 16 * 8 * (blk // 2 + 1)
            for m in range(2):
                ps = P1[:, m * 512:(m + 1) * 512]
                for kc in range(KC):
                    w = []
                    if kc == 0:
                        w.append(xw)
                        w.append(("aev", sch.n("aev")))
                        if blk == 0 and m == 0:
                            w.append(("ldg", LDG_A1))
                    sch.emit("tensor", "matmul", ps, wtile("wkv_r", kc, m * 128, 128, 256),
                             a16.ap(r_xblk, 0, 128, bb * 4096 + kc * 512, 512),
                             start=(kc == 0), stop=(kc == KC - 1), waits=w,
                             inc=("amm" if kc == KC - 1 else None))
                sch.emit("scalar", "activation",
                         a16.ap(r_ckv1, 0, 128, m * S + blk * 512, 512), ps,
                         AF.Gelu, bias=pvec("bkv", m),
                         waits=[("amm", sch.n("amm"))], inc="aev")
            psk = P2[:, (bb) * 512:(bb + 1) * 512]
            for kc in range(KC):
                w = []
                if kc == 0:
                    w.append(xw)
                    w.append(("rev", max(REVQ, sch.n("rev") - 1)))
                sch.emit("tensor", "matmul", psk, wtile("wkr_r", kc, 0, 128, 128),
                         a16.ap(r_xblk, 0, 128, bb * 4096 + kc * 512, 512),
                         start=(kc == 0), stop=(kc == KC - 1), waits=w,
                         inc=("amm" if kc == KC - 1 else None))
            sch.emit("scalar", "activation", a32.ap(r_krp, 0, 128, bb * 512, 512), psk,
                     AF.Identity, bias=pvec("bkr", 0),
                     waits=[("amm", sch.n("amm"))], inc="kpe")
            csw = ("csa" if bb == 0 else "csb"), 16 * 2 * (blk // 2 + 1)
            x1a = a32.ap(r_krp, 0, 64, bb * 512, 512)
            x2a = a32.ap(r_krp, 64, 128, bb * 512, 512)
            cos_t = a32.ap(r_csk, 0, 64, bb * 1024, 512)
            cos_b = a32.ap(r_csk, 64, 128, bb * 1024, 512)
            sin_t64 = a32.ap(r_csk, 64, 128, bb * 1024 + 512, 512)
            sin_b0 = a32.ap(r_csk, 0, 64, bb * 1024 + 512, 512)
            ta = a32.ap(r_rtmp, 0, 64, 0, 512)
            tb = a32.ap(r_rtmp, 0, 64, 512, 512)
            tc = a32.ap(r_rtmp, 64, 128, 0, 512)
            td = a32.ap(r_rtmp, 64, 128, 512, 512)
            kout_t = a16.ap(r_kr1, 0, 64, blk * 512, 512)
            kout_b = a16.ap(r_kr1, 64, 128, blk * 512, 512)
            sch.emit("vector", "tensor_mul", ta, x1a, cos_t, waits=[("kpe", sch.n("kpe")), csw])
            sch.emit("vector", "tensor_mul", tb, x2a, sin_t64)
            sch.emit("vector", "tensor_sub", kout_t, ta, tb)
            sch.emit("vector", "tensor_mul", tc, x2a, cos_b)
            sch.emit("vector", "tensor_mul", td, x1a, sin_b0)
            sch.emit("vector", "tensor_add", kout_b, tc, td, inc="rev")
        a16.freer(r_xblk)
        a32.freer(r_csk)
        a32.freer(r_krp)
        a32.freer(r_rtmp)
        a16.freer(wsb.pop("wkv_r"))
        a16.freer(wsb.pop("wkr_r"))
        REV_ALL = sch.n("rev")
        sch.barrier()

        DBG = {}
        r_out1 = a16.alloc(8 * TQ, "out1", high=True)
        r_kp = a16.alloc(S, "Kpair")
        r_vp = a16.alloc(32 * 132, "Vpair")
        DBG["vpA"] = r_vp
        r_qp = a16.alloc(TQ, "Qpair")
        r_pt = a16.alloc(2 * QT, "PT")
        r_lbc = a32.alloc(QT, "lbc")
        r_ftmp = a32.alloc(QT, "ftmp", high=True)
        for _c in range(32):
            sch.emit("gpsimd", "memset", a16.ap(r_vp, 0, 128, _c * 132 + 64, 1),
                     1.0, inc="msk")
        MSKV_A = sch.n("msk")

        lrow0 = a32.ap(r_rows, 0, 1, 0, QT)
        lrow1 = a32.ap(r_rows, 0, 1, 2048, QT)
        ones64_p0 = a32.ap(r_ones, 0, 1, 0, 64)

        def emit_attention(blkname, kv_gen, q_gen, bias_v_slice, out1_r, mskv):
            for hp in range(8):
                kv_gen(hp)
                _dbgv(hp)
                q_gen(hp)
                gg = sch.n("gge")
                for qt in range(NQT):
                    j0 = qt * QCPT
                    seq = []
                    for half in (0, 1):
                        for kcc in range(0, j0 + QCPT):
                            if kcc < j0:
                                seq.append((half, kcc, 0, None))
                            else:
                                seq.append((half, kcc, kcc - j0, (mh0 if half == 0 else mh1)))
                    for head in (0, 1):
                        hb = 64 * head
                        fvh = f"fv{head}"
                        fv0h = sch.n(fvh)
                        av_plan = []
                        for (half, kcc, t, mask) in seq:
                            c0 = t * 128
                            segs = []
                            s0 = c0
                            while s0 < QT:
                                seg_end = min(((s0 // 512) + 1) * 512, QT)
                                segs.append((s0, seg_end))
                                s0 = seg_end
                            av_plan.append(segs)
                        last_for_bank = {}
                        for ci, segs in enumerate(av_plan):
                            for si, (a0, a1) in enumerate(segs):
                                last_for_bank[a0 // 512] = (ci, si)
                        first_av = {0: True, 1: True}
                        pexs = f"pex{head}"
                        smms = f"smm{head}"
                        for ci, (half, kcc, t, mask) in enumerate(seq):
                            c0 = t * 128
                            kcol = half * S // 2 + kcc * 128
                            vchunk = half * HKC + kcc
                            segs = av_plan[ci]
                            for si, (a0, a1) in enumerate(segs):
                                w = []
                                if si == 0:
                                    w.append(("sfree", sch.n("sfree")))
                                    if ci == 0 and head == 0:
                                        w.append(("gge", gg))
                                sch.emit("tensor", "matmul", P1[:, a0:a1],
                                         a16.ap(r_kp, hb, hb + 64, kcol, 128),
                                         a16.ap(r_qp, hb, hb + 64, qt * QT + a0, a1 - a0),
                                         start=True, stop=True, waits=w,
                                         inc=(smms if si == len(segs) - 1 else None))
                            if mask is not None:
                                sch.emit("vector", "tensor_tensor", P1[:, c0:c0 + 128],
                                         P1[:, c0:c0 + 128], mask, ALU.add,
                                         waits=[(smms, sch.n(smms)), ("ld", LD0)],
                                         inc="mvk")
                                expw = [("mvk", sch.n("mvk"))]
                            else:
                                expw = [(smms, sch.n(smms))]
                            sch.emit("scalar", "activation",
                                     a16.ap(r_pt, 0, 128, head * QT + c0, QT - c0),
                                     P1[:, c0:QT], AF.Exp, scale=0.125, bias=shiftv,
                                     waits=expw + ([("msk", MSK0)] if qt == 0 and ci == 0 else []),
                                     inc="sfree")
                            for si, (a0, a1) in enumerate(segs):
                                w = [("sfree", sch.n("sfree"))] if si == 0 else []
                                if first_av[a0 // 512]:
                                    if fv0h > 0:
                                        w.append((fvh, fv0h))
                                    if (mskv is not None and hp == 0 and qt == 0
                                            and head == 0 and ci == 0):
                                        w.append(("msk", mskv))
                                is_last = last_for_bank[a0 // 512] == (ci, si)
                                lastc = (ci == len(seq) - 1 and si == len(segs) - 1)
                                if head == 0:
                                    sch.emit("tensor", "matmul", P3[0:64, a0:a1],
                                             a16.ap(r_vp, 0, 128, vchunk * 132, 64),
                                             a16.ap(r_pt, 0, 128, head * QT + a0, a1 - a0),
                                             start=first_av[a0 // 512], stop=is_last, waits=w)
                                    sch.emit("tensor", "matmul", P4[0:1, a0:a1],
                                             onesbf_c,
                                             a16.ap(r_pt, 0, 128, head * QT + a0, a1 - a0),
                                             start=first_av[a0 // 512], stop=is_last,
                                             inc=("oav" if lastc else None))
                                else:
                                    sch.emit("tensor", "matmul", P4[64:128, a0:a1],
                                             a16.ap(r_vp, 0, 128, vchunk * 132 + 66, 64),
                                             a16.ap(r_pt, 0, 128, head * QT + a0, a1 - a0),
                                             start=first_av[a0 // 512], stop=is_last, waits=w)
                                    sch.emit("tensor", "matmul", P2[0:1, a0:a1],
                                             onesbf_c,
                                             a16.ap(r_pt, 0, 128, head * QT + a0, a1 - a0),
                                             start=first_av[a0 // 512], stop=is_last,
                                             inc=("oav" if lastc else None))
                                first_av[a0 // 512] = False
                        lr_h = lrow0 if head == 0 else lrow1
                        lsrc = P4[0:1, :] if head == 0 else P2[0:1, :]
                        sch.emit("vector", "tensor_copy", lr_h, lsrc,
                                 waits=[("oav", sch.n("oav"))], inc=f"fc{head}")
                        sch.emit("vector", "reciprocal", lr_h, lr_h, inc=f"fc{head}")
                    ch = hp
                    for head in (0, 1):
                        fch, fbh = f"fc{head}", f"fb{head}"
                        fdh, fvh2 = f"fd{head}", f"fv{head}"
                        lr_h = lrow0 if head == 0 else lrow1
                        onesl = ones64_p0
                        bq0, bq1 = (0, 64) if head == 0 else (64, 128)
                        bdst = P4 if head == 0 else P2
                        for sg in range(2):
                            w = []
                            if sg == 0:
                                w.append((fch, sch.n(fch)))
                                if sch.n(fdh) > 0:
                                    w.append((fdh, sch.n(fdh)))
                            sch.emit("tensor", "matmul",
                                     bdst[bq0:bq1, sg * 512:(sg + 1) * 512],
                                     onesl, lr_h[:, sg * 512:(sg + 1) * 512],
                                     start=True, stop=True, waits=w,
                                     inc=(fbh if sg == 1 else None))
                        sch.emit("vector", "tensor_copy",
                                 a32.ap(r_lbc, bq0, bq1, 0, QT), bdst[bq0:bq1, 0:QT],
                                 waits=[(fbh, sch.n(fbh))], inc=fdh)
                        Osrc = P3[0:64, :] if head == 0 else P4[64:128, :]
                        sch.emit("vector", "scalar_tensor_tensor",
                                 a32.ap(r_ftmp, bq0, bq1, 0, QT),
                                 Osrc, 0.0, a32.ap(r_lbc, bq0, bq1, 0, QT),
                                 ALU.add, ALU.mult, inc=fvh2)
                        sch.emit("vector", "tensor_scalar",
                                 a16.ap(out1_r, bq0, bq1, ch * TQ + qt * QT, QT),
                                 a32.ap(r_ftmp, bq0, bq1, 0, QT),
                                 bias_v_slice(2 * hp + head), None, ALU.add,
                                 inc="fin")

        def kv1_gen(hp):
            _dbgv0 = hp
            if hp < 7:
                for kb in range(8):
                    pb = kb % 2
                    ps = P2[:, pb * 512:(pb + 1) * 512]
                    for lc in range(2):
                        w = []
                        if lc == 0:
                            w.append(("gge", sch.n("gge")))
                            if kb == 0:
                                w.append(("fd1", sch.n("fd1")))
                            if hp == 0 and kb == 0:
                                w.append(("fin", sch.n("fin")))
                        sch.emit("tensor", "matmul", ps,
                                 wtile("wkc_r", lc, hp * 128, 128, 896),
                                 a16.ap(r_ckv1, 0, 128, lc * S + kb * 512, 512),
                                 start=(lc == 0), stop=(lc == 1), waits=w,
                                 inc=("gmm" if lc == 1 else None))
                    sch.emit("scalar", "activation", a16.ap(r_kp, 0, 128, kb * 512, 512),
                             ps, AF.Gelu, bias=pvec("bkc", hp),
                             waits=[("gmm", sch.n("gmm"))], inc="gge")
            else:
                for kb in range(4):
                    sch.emit("scalar", "activation", a16.ap(r_kp, 0, 128, kb * 1024, 1024),
                             a16.ap(r_kr1, 0, 128, kb * 1024, 1024), AF.Gelu,
                             waits=[("rev", REV_ALL)] if kb == 0 else [], inc="gge")
            for j in range(32):
                pb = j % 2
                ps = P2[:, pb * 512: pb * 512 + 128]
                for lc in range(2):
                    w = [("gge", sch.n("gge"))] if lc == 0 else []
                    if j == 0 and lc == 0 and hp == 7:
                        w.append(("fd1", sch.n("fd1")))
                    sch.emit("tensor", "matmul", ps,
                             a16.ap(r_ckv1, 0, 128, lc * S + j * 128, 128),
                             wtile("wv_r", lc, hp * 128, 128, 1024),
                             start=(lc == 0), stop=(lc == 1), waits=w,
                             inc=("gmm" if lc == 1 else None))
                sch.emit("scalar", "activation", a16.ap(r_vp, 0, 128, j * 132, 64),
                         ps[:, 0:64], AF.Copy,
                         waits=[("gmm", sch.n("gmm"))], inc="gge")
                sch.emit("scalar", "activation", a16.ap(r_vp, 0, 128, j * 132 + 66, 64),
                         ps[:, 64:128], AF.Copy, inc="gge")

        def _dbgv(hp):
            if not os.environ.get("KDBG"):
                return
            vchk = a16.ap(r_vp, 0, 128, 0, 4224)
            def cb(arr):
                import numpy as _np
                a = _np.asarray(arr, dtype=_np.float32)
                ones = a[:, 64::132]
                print(f"DBGV pair{hp}: ones==1 {(ones==1.0).sum()} bad {(ones!=1.0).sum()}",
                      flush=True)
            sch.prog["scalar"].append(("__dbg__", (cb, vchk), {}, [], None, 1))

        def q1_gen(hp):
            if hp < 7:
                for qb in range(4):
                    pb = qb % 2
                    ps = P2[:, pb * 512:(pb + 1) * 512]
                    for lc in range(2):
                        w = [("gge", sch.n("gge"))] if lc == 0 else []
                        sch.emit("tensor", "matmul", ps,
                                 wtile("wqc_r", lc, hp * 128, 128, 896),
                                 a16.ap(r_cq1, 0, 128, lc * TQ + qb * 512, 512),
                                 start=(lc == 0), stop=(lc == 1), waits=w,
                                 inc=("gmm" if lc == 1 else None))
                    sch.emit("scalar", "activation", a16.ap(r_qp, 0, 128, qb * 512, 512),
                             ps, AF.Gelu, bias=pvec("bqc", hp),
                             waits=[("gmm", sch.n("gmm"))], inc="gge")
            else:
                for qb in range(2):
                    sch.emit("scalar", "activation", a16.ap(r_qp, 0, 128, qb * 1024, 1024),
                             a16.ap(r_qr1, 0, 128, qb * 1024, 1024), AF.Gelu,
                             waits=[("rev", REV_ALL)] if qb == 0 else [], inc="gge")

        def bv_slice(h):
            return pvec("bv", h // 2, 64 * (h % 2), 64 * (h % 2) + 64)

        emit_attention("a1", kv1_gen, q1_gen, bv_slice, r_out1, MSKV_A)
        a16.freer(r_ckv1)
        a16.freer(r_kr1)
        a16.freer(r_cq1)
        a16.freer(r_qr1)
        a16.freer(wsb.pop("wqc_r"))
        a16.freer(wsb.pop("wkc_r"))
        a16.freer(wsb.pop("wv_r"))
        a16.freer(r_kp)
        a16.freer(r_vp)
        a16.freer(r_qp)
        a16.freer(r_pt)
        a32.freer(r_lbc)
        sch.barrier()

        if os.environ.get("KDBG2"):
            for c_ in range(8):
                sch.emit("gpsimd", "dma_start", out=o1_dbg[c_ * 128:(c_ + 1) * 128, :],
                         in_=a16.ap(r_out1, 0, 128, c_ * TQ, TQ),
                         waits=[("fin", sch.n("fin"))], inc="o1d", incn=16)
        r_wo = load_w("wo_r", 8, 1024, high=True)
        r_sq = a32.alloc(512, "sqscratch", high=True)
        r_bcA = a32.alloc(2048, "bcA", high=True)
        r_bcB = a32.alloc(2048, "bcB", high=True)

        def ln_block(r_u, dst_write, gamma, beta, rowsr, rslot_persist=None,
                     wait_sem=None, dst16=None, dst16_dram=None, dram_f32=None):
            for cg in range(4):
                sl = slice(cg * 512, (cg + 1) * 512)
                ps1 = P3[0:1, cg * 256:cg * 256 + 512] if False else P3[0:1, 0:512]
                ps2 = P4[0:1, 0:512]
                for c in range(8):
                    w = []
                    if c == 0:
                        w.append((wait_sem[0], wait_sem[1]))
                        if cg > 0:
                            w.append(("lse", sch.n("lse")))
                    sch.emit("tensor", "matmul", ps1, ones_c,
                             a32.ap(r_u, 0, 128, c * 2048 + cg * 512, 512),
                             start=(c == 0), stop=(c == 7), waits=w,
                             inc=("lsm" if c == 7 else None))
                for c in range(8):
                    sch.emit("scalar", "activation", a32.ap(r_sq, 0, 128, 0, 512),
                             a32.ap(r_u, 0, 128, c * 2048 + cg * 512, 512), AF.Square,
                             waits=([("lsm", sch.n("lsm"))] if c == 0 else
                                    [("sqe", sch.n("sqe"))]), inc="sqp")
                    sch.emit("tensor", "matmul", ps2, ones_c, a32.ap(r_sq, 0, 128, 0, 512),
                             start=(c == 0), stop=(c == 7),
                             waits=[("sqp", sch.n("sqp"))],
                             inc=("lsm" if c == 7 else "sqe"))
                sch.emit("vector", "tensor_copy", ROW(r_rows, 0, cg * 512, 512), ps1,
                         waits=[("lsm", sch.n("lsm"))], inc="lse")
                sch.emit("vector", "tensor_copy", ROW(r_rows, 1, cg * 512, 512), ps2,
                         inc="lse")
            Arow = ROW(r_rows, 0)
            Brow = ROW(r_rows, 1)
            Crow = ROW(r_rows, 2)
            sch.wait("vector", ("lse", sch.n("lse")))
            sch.emit("vector", "tensor_scalar_mul", Arow, Arow, 1.0 / 1024)
            sch.emit("vector", "tensor_scalar_mul", Brow, Brow, 1.0 / 1024)
            sch.emit("vector", "tensor_mul", Crow, Arow, Arow)
            sch.emit("vector", "tensor_sub", Brow, Brow, Crow)
            sch.emit("vector", "tensor_scalar_add", Brow, Brow, EPS, inc="lse")
            sch.emit("scalar", "activation", Brow, Brow, AF.Sqrt,
                     waits=[("lse", sch.n("lse"))], inc="lse")
            sch.wait("vector", ("lse", sch.n("lse")))
            sch.emit("vector", "reciprocal", Brow, Brow)
            sch.emit("vector", "scalar_tensor_tensor", Crow, Arow, -1.0, Brow,
                     ALU.mult, ALU.mult, inc="lse")
            if rslot_persist is not None:
                sch.emit("sync", "dma_start", out=rows_dram[0:1, :], in_=Brow,
                         waits=[("lse", sch.n("lse"))], inc="rwd", incn=16)
                sch.emit("sync", "dma_start", out=rows_dram[1:2, :], in_=Crow,
                         inc="rwd", incn=16)
            sch.wait("tensor", ("lse", sch.n("lse")))
            for sg in range(4):
                sch.emit("tensor", "matmul", P3[:, 0:512], ones_r,
                         Brow[:, sg * 512:(sg + 1) * 512], start=True, stop=True,
                         waits=[("bce", sch.n("bce"))] if sg > 0 else [])
                sch.emit("tensor", "matmul", P4[:, 0:512], ones_r,
                         Crow[:, sg * 512:(sg + 1) * 512], start=True, stop=True, inc="bcm")
                sch.emit("scalar", "activation", a32.ap(r_bcA, 0, 128, sg * 512, 512),
                         P3[:, 0:512], AF.Copy, waits=[("bcm", sch.n("bcm"))])
                sch.emit("scalar", "activation", a32.ap(r_bcB, 0, 128, sg * 512, 512),
                         P4[:, 0:512], AF.Copy, inc="bce")
            for c in range(8):
                uap = a32.ap(r_u, 0, 128, c * 2048, 2048)
                tap = a32.ap(r_sq, 0, 128, 0, 2048) if False else None
                dst_write(c, uap)
            return

        r_y1bf = a16.alloc(8 * TQ, "y1bf")

        def y1_write(c, uap):
            for cg in range(4):
                u = a32.ap(r_x0, 0, 128, c * 2048 + cg * 512, 512)
                A_ = a32.ap(r_bcA, 0, 128, cg * 512, 512)
                B_ = a32.ap(r_bcB, 0, 128, cg * 512, 512)
                t = a32.ap(r_sq, 0, 128, 0, 512)
                w = [("bce", sch.n("bce"))] if (c == 0 and cg == 0) else []
                sch.emit("vector", "tensor_mul", t, u, A_, waits=w)
                sch.emit("vector", "tensor_add", u, t, B_, inc="lna")
            sch.emit("sync", "dma_start", out=y1_dram[c * 128:(c + 1) * 128, :],
                     in_=a32.ap(r_x0, 0, 128, c * 2048, 2048),
                     waits=[("lna", sch.n("lna"))], inc="y1d", incn=16)
            sch.emit("scalar", "activation", a16.ap(r_y1bf, 0, 128, c * 2048, 2048),
                     a32.ap(r_x0, 0, 128, c * 2048, 2048), AF.Copy,
                     waits=[("lna", sch.n("lna"))], inc="y1b")

        for m in range(8):
            for sg in range(4):
                pb = (m * 4 + sg) % 2
                ps = (P1 if pb == 0 else P2)[:, 0:512]
                for kc in range(8):
                    w = []
                    if kc == 0:
                        w.append(("fin", sch.n("fin")))
                        w.append(("ldg", sch.n("ldg")))
                        if m * 4 + sg >= 2:
                            w.append(("pje", sch.n("pje") - 1))
                    sch.emit("tensor", "matmul", ps, wtile("wo_r", kc, m * 128, 128, 1024),
                             a16.ap(r_out1, 0, 128, kc * TQ + sg * 512, 512),
                             start=(kc == 0), stop=(kc == 7), waits=w,
                             inc=("pjm" if kc == 7 else None))
                sch.emit("scalar", "activation", a32.ap(r_ftmp, 0, 128, 0, 512), ps,
                         AF.Identity, bias=pvec("bo", m),
                         waits=[("pjm", sch.n("pjm")), ("pje", sch.n("pje"))], inc="pjs")
                sch.emit("vector", "tensor_add",
                         a32.ap(r_x0, 0, 128, m * 2048 + sg * 512, 512),
                         a32.ap(r_x0, 0, 128, m * 2048 + sg * 512, 512),
                         a32.ap(r_ftmp, 0, 128, 0, 512),
                         waits=[("pjs", sch.n("pjs"))], inc="pje")
        _lnw = ("pje", sch.n("pje"))
        if os.environ.get("KDBG2"):
            for c_ in range(8):
                sch.emit("sync", "dma_start", out=u1_dbg[c_ * 128:(c_ + 1) * 128, :],
                         in_=a32.ap(r_x0, 0, 128, c_ * 2048, 2048),
                         waits=[("pje", sch.n("pje"))], inc="u1d", incn=16)
            _lnw = ("u1d", sch.n("u1d"))
        ln_block(r_x0, y1_write, None, None, r_rows, wait_sem=_lnw)
        a16.freer(r_out1)
        a32.freer(r_x0)
        a16.freer(wsb.pop("wo_r"))
        sch.barrier()

        r_wdkv = load_w("wdkv_r", 8, 512)
        r_wdq = load_w("wdq_r", 8, 512)
        LDG_B = sch.n("ldg")
        r_ckv2o = a16.alloc(4 * TQ, "ckv2own")
        r_cq2 = a16.alloc(4 * TQ, "cq2")
        for (nm, dstr, bias, osem) in [("wdkv_r", r_ckv2o, "bdkv", "c2e"), ("wdq_r", r_cq2, "bdq", "c2e")]:
            for m in range(4):
                for sg in range(4):
                    pb = (m * 4 + sg) % 2
                    ps = (P1 if pb == 0 else P2)[:, 0:512]
                    for kc in range(8):
                        w = []
                        if kc == 0:
                            w.append(("y1b", sch.n("y1b")))
                            w.append(("ldg", LDG_B))
                            w.append(("pje", sch.n("pje")))
                            if sch.n("c2e") >= 2:
                                w.append(("c2e", sch.n("c2e") - 1))
                        sch.emit("tensor", "matmul", ps, wtile(nm, kc, m * 128, 128, 512),
                                 a16.ap(r_y1bf, 0, 128, kc * 2048 + sg * 512, 512),
                                 start=(kc == 0), stop=(kc == 7), waits=w,
                                 inc=("c2m" if kc == 7 else None))
                    sch.emit("scalar", "activation",
                             a16.ap(dstr, 0, 128, m * TQ + sg * 512, 512), ps,
                             AF.Identity, bias=pvec(bias, m),
                             waits=[("c2m", sch.n("c2m"))], inc="c2e")
        for m in range(4):
            sch.emit("sync", "dma_start", out=cc_in[m * 128:(m + 1) * 128, :],
                     in_=a16.ap(r_ckv2o, 0, 128, m * TQ, TQ),
                     waits=[("c2e", 4 * (m + 1))], inc="cci", incn=16)
        sch.emit("gpsimd", "collective_compute", "AllGather", ALU.bypass,
                 replica_groups=[[0, 1], [2, 3], [4, 5], [6, 7]],
                 ins=[cc_in.ap().opt()], outs=[cc_out.ap().opt()],
                 waits=[("cci", sch.n("cci"))], inc="ccd")
        r_ckv2f = a16.alloc(4 * S, "ckv2f")
        for m in range(4):
            for e_ in range(2):
                sch.emit("sync", "dma_start",
                         out=a16.ap(r_ckv2f, 0, 128, m * S + e_ * TQ, TQ),
                         in_=cc_out[e_ * 512 + m * 128: e_ * 512 + m * 128 + 128, :],
                         waits=[("ccd", 1)], inc="ccr", incn=16)
        a16.freer(r_ckv2o)
        sch.wait("gpsimd", ("cci", sch.n("cci")))
        r_wuk = load_w("wuk_r", 4, 1024)
        r_wuv = load_w("wuv_r", 4, 1024)
        r_wuq = load_w("wuq_r", 4, 1024)
        LDG_B2 = sch.n("ldg")
        r_kp = a16.alloc(S, "Kpair")
        r_vp = a16.alloc(32 * 132, "Vpair")
        r_qp = a16.alloc(TQ, "Qpair")
        r_pt = a16.alloc(2 * QT, "PT")
        r_lbc = a32.alloc(QT, "lbc")
        for _c in range(32):
            sch.emit("gpsimd", "memset", a16.ap(r_vp, 0, 128, _c * 132 + 64, 1),
                     1.0, inc="msk")
        MSKV_B = sch.n("msk")

        def kv2_gen(hp):
            for kb in range(8):
                pb = kb % 2
                ps = P2[:, pb * 512:(pb + 1) * 512]
                for lc in range(4):
                    w = []
                    if lc == 0:
                        w.append(("gge", sch.n("gge")))
                        if kb == 0:
                            w.append(("fd1", sch.n("fd1")))
                        if hp == 0 and kb == 0:
                            w += [("ccr", 16 * 8), ("ldg", LDG_B2), ("fin", sch.n("fin"))]
                    sch.emit("tensor", "matmul", ps,
                             wtile("wuk_r", lc, hp * 128, 128, 1024),
                             a16.ap(r_ckv2f, 0, 128, lc * S + kb * 512, 512),
                             start=(lc == 0), stop=(lc == 3), waits=w,
                             inc=("gmm" if lc == 3 else None))
                sch.emit("scalar", "activation", a16.ap(r_kp, 0, 128, kb * 512, 512),
                         ps, AF.Identity, bias=pvec("buk", hp),
                         waits=[("gmm", sch.n("gmm"))], inc="gge")
            for j in range(32):
                pb = j % 2
                ps = P2[:, pb * 512: pb * 512 + 128]
                for lc in range(4):
                    w = [("gge", sch.n("gge"))] if lc == 0 else []
                    sch.emit("tensor", "matmul", ps,
                             a16.ap(r_ckv2f, 0, 128, lc * S + j * 128, 128),
                             wtile("wuv_r", lc, hp * 128, 128, 1024),
                             start=(lc == 0), stop=(lc == 3), waits=w,
                             inc=("gmm" if lc == 3 else None))
                sch.emit("scalar", "activation", a16.ap(r_vp, 0, 128, j * 132, 64),
                         ps[:, 0:64], AF.Copy,
                         waits=[("gmm", sch.n("gmm"))], inc="gge")
                sch.emit("scalar", "activation", a16.ap(r_vp, 0, 128, j * 132 + 66, 64),
                         ps[:, 64:128], AF.Copy, inc="gge")

        def q2_gen(hp):
            for qb in range(4):
                pb = qb % 2
                ps = P2[:, pb * 512:(pb + 1) * 512]
                for lc in range(4):
                    w = [("gge", sch.n("gge"))] if lc == 0 else []
                    sch.emit("tensor", "matmul", ps,
                             wtile("wuq_r", lc, hp * 128, 128, 1024),
                             a16.ap(r_cq2, 0, 128, lc * TQ + qb * 512, 512),
                             start=(lc == 0), stop=(lc == 3), waits=w,
                             inc=("gmm" if lc == 3 else None))
                sch.emit("scalar", "activation", a16.ap(r_qp, 0, 128, qb * 512, 512),
                         ps, AF.Identity, bias=pvec("buq", hp),
                         waits=[("gmm", sch.n("gmm"))], inc="gge")

        def buv_slice(h):
            return pvec("buv", h // 2, 64 * (h % 2), 64 * (h % 2) + 64)

        a16.freer(r_y1bf)
        a16.freer(wsb.pop("wdkv_r"))
        a16.freer(wsb.pop("wdq_r"))
        r_out2 = a16.alloc(8 * TQ, "out2", high=True)
        emit_attention("a2", kv2_gen, q2_gen, buv_slice, r_out2, MSKV_B)
        a16.freer(r_ckv2f)
        a16.freer(r_cq2)
        a16.freer(r_kp)
        a16.freer(r_vp)
        a16.freer(r_qp)
        a16.freer(r_pt)
        a16.freer(wsb.pop("wuk_r"))
        a16.freer(wsb.pop("wuv_r"))
        a16.freer(wsb.pop("wuq_r"))
        a32.freer(r_lbc)
        sch.barrier()

        r_x1bf = a16.alloc(8 * TQ, "x1bf", high=True)
        r_wo2 = load_w("wo2_r", 8, 1024, high=True)
        LDG_B3 = sch.n("ldg")
        r_u2 = a32.alloc(8 * 2048, "u2")
        r_y1c = r_u2
        for m in range(8):
            sch.emit("sync", "dma_start", out=a32.ap(r_u2, 0, 128, m * 2048, 2048),
                     in_=y1_dram[m * 128:(m + 1) * 128, :],
                     waits=[("y1d", sch.n("y1d"))], inc="y1r", incn=16)
        for m in range(8):
            for sg in range(4):
                pb = (m * 4 + sg) % 2
                ps = (P1 if pb == 0 else P2)[:, 0:512]
                for kc in range(8):
                    w = []
                    if kc == 0:
                        w.append(("fin", sch.n("fin")))
                        w.append(("ldg", LDG_B3))
                        w.append(("y1r", 16 * 8))
                        if m * 4 + sg >= 2:
                            w.append(("pje", sch.n("pje") - 1))
                    sch.emit("tensor", "matmul", ps, wtile("wo2_r", kc, m * 128, 128, 1024),
                             a16.ap(r_out2, 0, 128, kc * TQ + sg * 512, 512),
                             start=(kc == 0), stop=(kc == 7), waits=w,
                             inc=("pjm" if kc == 7 else None))
                sch.emit("scalar", "activation", a32.ap(r_ftmp, 0, 128, 0, 512), ps,
                         AF.Identity, bias=pvec("bo2", m),
                         waits=[("pjm", sch.n("pjm")), ("pje", sch.n("pje"))], inc="pjs")
                sch.emit("vector", "tensor_add",
                         a32.ap(r_u2, 0, 128, m * 2048 + sg * 512, 512),
                         a32.ap(r_u2, 0, 128, m * 2048 + sg * 512, 512),
                         a32.ap(r_ftmp, 0, 128, 0, 512),
                         waits=[("pjs", sch.n("pjs"))], inc="pje")

        def x1_write(c, uap):
            for cg in range(4):
                u = a32.ap(r_u2, 0, 128, c * 2048 + cg * 512, 512)
                A_ = a32.ap(r_bcA, 0, 128, cg * 512, 512)
                B_ = a32.ap(r_bcB, 0, 128, cg * 512, 512)
                t = a32.ap(r_sq, 0, 128, 0, 512)
                w = [("bce", sch.n("bce"))] if (c == 0 and cg == 0) else []
                w.append(("x1b", sch.n("x1b")))
                sch.emit("vector", "tensor_mul", t, u, A_, waits=w)
                sch.emit("vector", "tensor_add", t, t, B_, inc="lna")
                sch.emit("scalar", "activation",
                         a16.ap(r_x1bf, 0, 128, c * 2048 + cg * 512, 512), t,
                         AF.Identity, scale=pvec("g1", c), bias=pvec("be1", c),
                         waits=[("lna", sch.n("lna"))], inc="x1b")
            sch.emit("sync", "dma_start", out=u2_dram[c * 128:(c + 1) * 128, :],
                     in_=a32.ap(r_u2, 0, 128, c * 2048, 2048),
                     waits=[("pje", sch.n("pje"))], inc="u2d", incn=16)
            sch.emit("sync", "dma_start", out=x1bf_dram[c * 128:(c + 1) * 128, :],
                     in_=a16.ap(r_x1bf, 0, 128, c * 2048, 2048),
                     waits=[("x1b", sch.n("x1b"))], inc="x1d", incn=16)

        ln_block(r_u2, x1_write, None, None, r_rows, rslot_persist=True,
                 wait_sem=("pje", sch.n("pje")))
        a16.freer(r_out2)
        a16.freer(wsb.pop("wo2_r"))
        sch.barrier()

        r_wg = load_w("wg_r", 8, 8)
        r_wn = load_w("wn_r", 8, 8)
        r_bg = a32.alloc(8, "bg8", high=True)
        r_bn = a32.alloc(8, "bn8", high=True)
        r_noise = a32.alloc(NQC * 8, "noise")
        r_b2sb = a32.alloc(1024, "b2sb", high=True)
        r_gT = a32.alloc(2048, "gT", high=True)
        DBG["gT"] = r_gT
        r_oneh = a32.alloc(1024, "onehot", high=True)
        r_ident = a32.alloc(128, "ident")
        r_gtk = a32.alloc(16 * 8, "g_tokmaj")
        sch.emit("sync", "dma_start", out=a32.ap(r_bg, 0, 1), in_=din["bg8"], inc="ld", incn=16)
        sch.emit("sync", "dma_start", out=a32.ap(r_bn, 0, 1), in_=din["bn8"], inc="ld", incn=16)
        sch.emit("sync", "dma_start", out=a32.ap(r_noise), in_=din["noise_tm"], inc="ld", incn=16)
        sch.emit("sync", "dma_start", out=a32.ap(r_b2sb, 0, 8), in_=din["b2"], inc="ld", incn=16)
        sch.emit("gpsimd", "iota", a32.ap(r_oneh, 0, 8), [[1, 8], [0, 128]],
                 channel_multiplier=-1, allow_small_or_imprecise_dtypes=True, inc="msk")
        sch.emit("gpsimd", "iota", a32.ap(r_ident), [[1, 128]],
                 channel_multiplier=-1, allow_small_or_imprecise_dtypes=True, inc="msk")
        sch.wait("vector", ("msk", sch.n("msk")))
        sch.emit("vector", "tensor_scalar", a32.ap(r_oneh, 0, 8), a32.ap(r_oneh, 0, 8),
                 0.0, None, ALU.is_equal)
        sch.emit("vector", "tensor_scalar", a32.ap(r_ident), a32.ap(r_ident),
                 0.0, None, ALU.is_equal, inc="msk")
        LDGT = sch.n("ld")
        MSKG = sch.n("msk")
        LDG_G = sch.n("ldg")
        for tc in range(16):
            psg = P1[:, 0:8]
            psn = P1[:, 512:520]
            for which, (wnm, bres, ps) in enumerate([("wg_r", r_bg, psg), ("wn_r", r_bn, psn)]):
                for kc in range(8):
                    w = []
                    if kc == 0:
                        w.append(("x1b", sch.n("x1b")))
                        w.append(("ldg", LDG_G))
                        w.append(("ld", LDGT))
                        if tc > 0:
                            w.append(("gte", sch.n("gte")))
                            if os.environ.get("KDBGHW") and which == 0:
                                w.append(("dbghw", sch.n("dbghw")))
                    sch.emit("tensor", "matmul", ps,
                             a16.ap(r_x1bf, 0, 128, kc * 2048 + tc * 128, 128),
                             wtile(wnm, kc, 0, 8, 8),
                             start=(kc == 0), stop=False, waits=w)
                sch.emit("tensor", "matmul", ps, ones_r, a32.ap(bres, 0, 1),
                         start=False, stop=True, inc="gtm")
            hx = a32.ap(r_gtk, 0, 128, 0, 8)
            hn = a32.ap(r_gtk, 0, 128, 8, 8)
            sch.emit("scalar", "activation", hx, psg, AF.Copy,
                     waits=[("gtm", sch.n("gtm"))], inc="gts")
            sch.emit("scalar", "activation", hn, psn, AF.Copy, inc="gts")
            t1 = a32.ap(r_gtk, 0, 128, 16, 8)
            t2 = a32.ap(r_gtk, 0, 128, 24, 8)
            if os.environ.get("KDBGHW"):
                sch.emit("sync", "dma_start",
                         out=att_dbg[:, 10000 + tc * 40: 10000 + tc * 40 + 8],
                         in_=hx, waits=[("gts", sch.n("gts"))], inc="dbg2", incn=16)
            sch.emit("scalar", "activation", t1, hn, AF.Abs,
                     waits=[("gts", sch.n("gts"))], inc="gss")
            sch.emit("scalar", "activation", t1, t1, AF.Exp, scale=-1.0,
                     waits=[("gss", sch.n("gss"))], inc="gss")
            sch.emit("scalar", "activation", t1, t1, AF.Ln, bias=1.0,
                     waits=[("gss", sch.n("gss"))], inc="gts")
            sch.wait("vector", ("gts", sch.n("gts")))
            if os.environ.get("KDBGHW"):
                sch.emit("sync", "dma_start",
                         out=att_dbg[:, 10000 + tc * 40 + 8: 10000 + tc * 40 + 16],
                         in_=t1, waits=[("gts", sch.n("gts"))], inc="dbg2", incn=16)
                sch.wait("vector", ("dbg2", sch.n("dbg2")))
            nz = a32.ap(r_noise, 0, 128, tc * 8, 8)
            def vstep0(meth, *aa, **kk):
                sch.emit("vector", meth, *aa, waits=[("gtv", sch.n("gtv"))], inc="gtv", **kk)
            vstep0("tensor_scalar_max", t2, hn, 0.0)
            vstep0("tensor_add", t1, t1, t2)
            vstep0("tensor_mul", t2, nz, t1)
            vstep0("tensor_add", hx, hx, t2)
            m1 = a32.ap(r_gtk, 0, 128, 32, 1)
            m2 = a32.ap(r_gtk, 0, 128, 33, 1)
            d12 = a32.ap(r_gtk, 0, 128, 34, 1)
            w1 = a32.ap(r_gtk, 0, 128, 35, 1)
            w2 = a32.ap(r_gtk, 0, 128, 36, 1)
            e1 = a32.ap(r_gtk, 0, 128, 40, 8)
            e2 = a32.ap(r_gtk, 0, 128, 48, 8)
            g8 = a32.ap(r_gtk, 0, 128, 56, 8)
            def vstep(meth, *aa, **kk):
                sch.emit("vector", meth, *aa, waits=[("gtv", sch.n("gtv"))], inc="gtv", **kk)
            vstep("reduce_max", m1, hx, axis=AXX)
            vstep("tensor_scalar", e1, hx, m1, None, ALU.is_ge)
            vstep("scalar_tensor_tensor", e2, e1, NEGM, hx, ALU.mult, ALU.add)
            vstep("reduce_max", m2, e2, axis=AXX)
            vstep("tensor_sub", d12, m2, m1)
            sch.emit("scalar", "activation", w1, d12, AF.Sigmoid, scale=-1.0,
                     waits=[("gtv", sch.n("gtv"))], inc="gts")
            sch.wait("vector", ("gts", sch.n("gts")))
            vstep("tensor_scalar", w2, w1, -1.0, 1.0, ALU.mult, ALU.add)
            vstep("tensor_scalar", e2, e2, m2, None, ALU.is_ge)
            vstep("tensor_scalar", g8, e1, w1, None, ALU.mult)
            vstep("scalar_tensor_tensor", g8, e2, w2, g8, ALU.mult, ALU.add)
            if os.environ.get("KDBGHW"):
                sch.emit("sync", "dma_start",
                         out=att_dbg[:, 2048 + tc * 8: 2048 + tc * 8 + 8],
                         in_=hx, waits=[("gtv", sch.n("gtv"))],
                         inc="dbghw", incn=16)
                sch.emit("sync", "dma_start",
                         out=att_dbg[:, 4096 + tc * 24: 4096 + tc * 24 + 8],
                         in_=nz, inc="dbghw", incn=16)
                sch.emit("sync", "dma_start",
                         out=att_dbg[:, 4096 + tc * 24 + 8: 4096 + tc * 24 + 16],
                         in_=t1, inc="dbghw", incn=16)
                sch.emit("sync", "dma_start",
                         out=att_dbg[:, 4096 + tc * 24 + 16: 4096 + tc * 24 + 24],
                         in_=hn, inc="dbghw", incn=16)
                sch.emit("sync", "dma_start",
                         out=att_dbg[:, 8192 + tc * 8: 8192 + tc * 8 + 8],
                         in_=g8, inc="dbghw", incn=16)
                sch.emit("sync", "dma_start",
                         out=att_dbg[:, 10000 + tc * 40 + 16: 10000 + tc * 40 + 24],
                         in_=t2, inc="dbghw", incn=16)
            sch.wait("tensor", ("gtv", sch.n("gtv")), ("msk", MSKG))
            sch.emit("tensor", "transpose", P2[0:8, 0:128], g8, a32.ap(r_ident),
                     inc="gtm")
            sch.emit("scalar", "activation", a32.ap(r_gT, 0, 8, tc * 128, 128),
                     P2[0:8, 0:128], AF.Copy,
                     waits=[("gtm", sch.n("gtm"))], inc="gte")
        if os.environ.get("KDBGHW"):
            sch.emit("sync", "dma_start", out=att_dbg[0:8, 0:2048], in_=a32.ap(r_gT, 0, 8),
                     waits=[("gte", sch.n("gte"))], inc="dbghw", incn=16)
            sch.wait("sync", ("dbghw", sch.n("dbghw")))
        a32.freer(r_u2)
        a32.freer(r_noise)
        a16.freer(r_x1bf)
        a16.freer(wsb.pop("wg_r"))
        a16.freer(wsb.pop("wn_r"))
        a32.freer(r_mh0)
        a32.freer(r_mh1)
        a32.freer(r_ident)
        a32.freer(r_gtk)
        a32.freer(r_bcA)
        a32.freer(r_bcB)
        sch.barrier()
        r_bcA = a32.alloc(1024, "bcA", high=True)
        r_bcB = a32.alloc(1024, "bcB", high=True)

        TT = 1024
        r_hsb = a16.alloc(MC * TT, "hsb")
        r_acc = a32.alloc(8 * TT, "acc")
        r_w1b = a16.alloc(2 * 2048, "w1buf")
        r_w2b = a16.alloc(2 * MC * 128, "w2buf")
        r_Gbc = a32.alloc(TT, "Gbc")
        r_b1 = a32.alloc(E * MC, "b1p")
        r_x1t = a16.alloc(8 * TT, "x1tile", high=True)
        sch.emit("sync", "dma_start", out=a32.ap(r_b1), in_=din["b1p"], inc="ld", incn=16)
        LDB1 = sch.n("ld")
        GTE_ALL = sch.n("gte")

        w1_dma = 0
        w2_dma = 0
        n_out_round = 0
        hr_glob = 0
        for tile in range(2):
            for kc in range(8):
                sch.emit("sync", "dma_start", out=a16.ap(r_x1t, 0, 128, kc * TT, TT),
                         in_=x1bf_dram[kc * 128:(kc + 1) * 128, tile * TT:(tile + 1) * TT],
                         waits=([("x1d", 16 * 8)] if kc == 0 else []) +
                               ([("acs", sch.n("acs"))] if tile > 0 and kc == 0 else []),
                         inc="x1t", incn=16)
            X1T = sch.n("x1t")
            for dc in range(8):
                buf = dc % 2
                po = (P3 if buf == 0 else P4)
                for seg in range(2):
                    w = []
                    if dc == 0 and seg == 0:
                        w += [("ld", LDB1), ("gte", GTE_ALL), ("fin", sch.n("fin"))]
                        if tile > 0:
                            w.append(("acs", sch.n("acs")))
                    if dc >= 2 and seg == 0:
                        w.append(("oev", sch.n("oev") - 1))
                    sch.emit("tensor", "matmul", po[:, seg * 512:(seg + 1) * 512],
                             a32.ap(r_b2sb, 0, 8, dc * 128, 128),
                             a32.ap(r_gT, 0, 8, tile * TT + seg * 512, 512),
                             start=True, stop=True, waits=w,
                             inc=("omm" if seg == 1 else None))
                n_out_round += 1
                sch.emit("vector", "tensor_copy", a32.ap(r_acc, 0, 128, dc * TT, TT),
                         po[:, :], waits=[("omm", sch.n("omm"))], inc="oev")
            for e in range(E):
                for seg in range(2):
                    gw = [("oev", sch.n("oev"))] if seg == 0 else []
                    sch.emit("tensor", "matmul", P3[:, seg * 512:(seg + 1) * 512],
                             a32.ap(r_oneh, 0, 8, e * 128, 128),
                             a32.ap(r_gT, 0, 8, tile * TT + seg * 512, 512),
                             start=True, stop=True, waits=gw,
                             inc=("gbm" if seg == 1 else None))
                sch.emit("scalar", "activation", a32.ap(r_Gbc), P3[:, :], AF.Copy,
                         waits=[("gbm", sch.n("gbm"))], inc="gbe")
                for mg in range(MC // 2):
                    sbi = w1_dma % 2
                    dst = a16.ap(r_w1b, 0, 128, sbi * 2048, 2048)
                    waits = [("hmm", 2 * w1_dma - 2)] if w1_dma >= 2 else []
                    w1sem = "wl1a" if w1_dma % 2 == 0 else "wl1b"
                    sch.emit("gpsimd", "dma_start", out=dst,
                             in_=din["w1r"][e, mg].rearrange("p kc m2 mi -> p (kc m2 mi)"),
                             waits=waits, inc=w1sem, incn=16)
                    w1_dma += 1
                    for m2 in range(2):
                        m = 2 * mg + m2
                        hb_ = m % 2
                        hp_ = (P1 if hb_ == 0 else P2)
                        for kc in range(KC):
                            lhsT = a16.ap(r_w1b, 0, 128, sbi * 2048 + kc * 256 + m2 * 128, 128)
                            for seg in range(2):
                                w = []
                                if kc == 0 and seg == 0:
                                    w.append((("wl1a" if (w1_dma - 1) % 2 == 0 else "wl1b"),
                                              16 * (((w1_dma - 1) // 2) + 1)))
                                    if e == 0 and m == 0:
                                        w.append(("x1t", X1T))
                                        if tile == 0:
                                            w.append(("gte", GTE_ALL))
                                    if hr_glob >= 2:
                                        w.append(("hev", hr_glob - 1))
                                sch.emit("tensor", "matmul",
                                         hp_[:, seg * 512:(seg + 1) * 512], lhsT,
                                         a16.ap(r_x1t, 0, 128, kc * TT + seg * 512, 512),
                                         start=(kc == 0), stop=(kc == KC - 1), waits=w,
                                         inc=("hmm" if (kc == KC - 1 and seg == 1) else None))
                        hr_glob += 1
                        sch.emit("scalar", "activation", a16.ap(r_hsb, 0, 128, m * TT, TT),
                                 hp_[:, :], AF.Relu,
                                 bias=a32.ap(r_b1, 0, 128, e * MC + m, 1),
                                 waits=[("hmm", sch.n("hmm"))], inc="hev")
                for dc in range(8):
                    j = w2_dma
                    sbi = j % 2
                    dst = a16.ap(r_w2b, 0, 128, sbi * MC * 128, MC * 128)
                    waits = [("omm", sch.n("omm") - 1)] if j >= 2 else []
                    w2sem = "wl2a" if j % 2 == 0 else "wl2b"
                    sch.emit("gpsimd", "dma_start", out=dst,
                             in_=din["w2r"][e, dc].rearrange("p mc di -> p (mc di)"),
                             waits=waits, inc=w2sem, incn=16)
                    w2_dma += 1
                    ob = n_out_round % 2
                    po = (P3 if ob == 0 else P4)
                    for mc in range(MC):
                        lhsT = a16.ap(r_w2b, 0, 128, sbi * MC * 128 + mc * 128, 128)
                        for seg in range(2):
                            w = []
                            if mc == 0 and seg == 0:
                                w.append((("wl2a" if (w2_dma - 1) % 2 == 0 else "wl2b"),
                                          16 * (((w2_dma - 1) // 2) + 1)))
                                w.append(("hev", sch.n("hev")))
                                if n_out_round >= 2:
                                    w.append(("oev", sch.n("oev") - 1))
                                if dc == 0:
                                    w.append(("gbe", sch.n("gbe")))
                            sch.emit("tensor", "matmul", po[:, seg * 512:(seg + 1) * 512],
                                     lhsT, a16.ap(r_hsb, 0, 128, mc * TT + seg * 512, 512),
                                     start=(mc == 0), stop=(mc == MC - 1), waits=w,
                                     inc=("omm" if (mc == MC - 1 and seg == 1) else None))
                    n_out_round += 1
                    sch.emit("vector", "tensor_tensor", a32.ap(r_ftmp, 0, 128, 0, TT),
                             po[:, :], a32.ap(r_Gbc), ALU.mult,
                             waits=[("omm", sch.n("omm")), ("gbe", sch.n("gbe"))])
                    sch.emit("vector", "tensor_add", a32.ap(r_acc, 0, 128, dc * TT, TT),
                             a32.ap(r_acc, 0, 128, dc * TT, TT),
                             a32.ap(r_ftmp, 0, 128, 0, TT), inc="oev")

            if tile == 0:
                r_fu = a32.alloc(TT, "finale_u")
            sch.emit("sync", "dma_start", out=ROW(r_rows, 2, 0, TT),
                     in_=rows_dram[0:1, tile * TT:(tile + 1) * TT],
                     waits=[("rwd", 32), ("fcv", sch.n("fcv"))], inc="prl", incn=16)
            sch.emit("sync", "dma_start", out=ROW(r_rows, 2, TT, 2048 - TT)
                     if False else ROW(r_rows, 0, 1024, TT),
                     in_=rows_dram[1:2, tile * TT:(tile + 1) * TT],
                     inc="prl", incn=16)
            for sg in range(2):
                w = [("oev", sch.n("oev")), ("prl", sch.n("prl"))] if sg == 0 else []
                sch.emit("tensor", "matmul", P1[:, sg * 512:(sg + 1) * 512], ones_r,
                         ROW(r_rows, 2, sg * 512, 512),
                         start=True, stop=True, waits=w)
                sch.emit("tensor", "matmul", P2[:, sg * 512:(sg + 1) * 512], ones_r,
                         ROW(r_rows, 0, 1024 + sg * 512, 512),
                         start=True, stop=True, inc="bcm")
            sch.emit("scalar", "activation", a32.ap(r_bcA, 0, 128, 0, TT), P1[:, 0:TT], AF.Copy,
                     waits=[("bcm", sch.n("bcm"))])
            sch.emit("scalar", "activation", a32.ap(r_bcB, 0, 128, 0, TT), P2[:, 0:TT], AF.Copy,
                     inc="bce")
            BCE2 = sch.n("bce")
            for c in range(8):
                fs = "fua"
                u2c = a32.ap(r_fu, 0, 128, 0, TT)
                sch.emit("sync", "dma_start", out=u2c,
                         in_=u2_dram[c * 128:(c + 1) * 128, tile * TT:(tile + 1) * TT],
                         waits=[("u2d", 16 * 8)] + ([("fcv", sch.n("fcv"))] if c >= 1 or tile > 0 else []),
                         inc=fs, incn=16)
                accc = a32.ap(r_acc, 0, 128, c * TT, TT)
                t = a32.ap(r_ftmp, 0, 128, 0, TT)
                sch.wait("vector", (fs, 16 * sch.n(fs) // 16),
                         ("oev", sch.n("oev")), ("bce", BCE2))
                sch.emit("vector", "tensor_mul", t, u2c, a32.ap(r_bcA, 0, 128, 0, TT))
                sch.emit("vector", "tensor_add", t, t, a32.ap(r_bcB, 0, 128, 0, TT), inc="fcv")
                sch.emit("scalar", "activation", u2c, t, AF.Identity,
                         scale=pvec("g1", c), bias=pvec("be1", c),
                         waits=[("fcv", sch.n("fcv"))], inc="fcs")
                sch.wait("vector", ("fcs", sch.n("fcs")))
                sch.emit("vector", "tensor_add", accc, accc, u2c, inc="fcv")
            fin_stats_emitted = []

            def fin_stats(r_src, ncols):
                for cg in range(ncols // 512):
                    for c in range(8):
                        w = []
                        if c == 0:
                            w.append(("fcv", sch.n("fcv")))
                            if cg > 0:
                                w.append(("lse", sch.n("lse")))
                        sch.emit("tensor", "matmul", P1[0:1, 0:512], ones_c,
                                 a32.ap(r_src, 0, 128, c * ncols + cg * 512, 512),
                                 start=(c == 0), stop=(c == 7), waits=w,
                                 inc=("lsm" if c == 7 else None))
                    for c in range(8):
                        sch.emit("scalar", "activation", a32.ap(r_sq, 0, 128, 0, 512),
                                 a32.ap(r_src, 0, 128, c * ncols + cg * 512, 512), AF.Square,
                                 waits=([("lsm", sch.n("lsm"))] if c == 0 else
                                        [("sqe", sch.n("sqe"))]), inc="sqp")
                        sch.emit("tensor", "matmul", P2[0:1, 0:512], ones_c,
                                 a32.ap(r_sq, 0, 128, 0, 512),
                                 start=(c == 0), stop=(c == 7),
                                 waits=[("sqp", sch.n("sqp"))],
                                 inc=("lsm" if c == 7 else "sqe"))
                    sch.emit("vector", "tensor_copy", ROW(r_rows, 0, cg * 512, 512),
                             P1[0:1, 0:512], waits=[("lsm", sch.n("lsm"))], inc="lse")
                    sch.emit("vector", "tensor_copy", ROW(r_rows, 1, cg * 512, 512),
                             P2[0:1, 0:512], inc="lse")
                Arow = ROW(r_rows, 0, 0, ncols)
                Brow = ROW(r_rows, 1, 0, ncols)
                Crow = ROW(r_rows, 2, 0, ncols)
                sch.wait("vector", ("lse", sch.n("lse")))
                sch.emit("vector", "tensor_scalar_mul", Arow, Arow, 1.0 / 1024)
                sch.emit("vector", "tensor_scalar_mul", Brow, Brow, 1.0 / 1024)
                sch.emit("vector", "tensor_mul", Crow, Arow, Arow)
                sch.emit("vector", "tensor_sub", Brow, Brow, Crow)
                sch.emit("vector", "tensor_scalar_add", Brow, Brow, EPS, inc="lse")
                sch.emit("scalar", "activation", Brow, Brow, AF.Sqrt,
                         waits=[("lse", sch.n("lse"))], inc="lse")
                sch.wait("vector", ("lse", sch.n("lse")))
                sch.emit("vector", "reciprocal", Brow, Brow)
                sch.emit("vector", "scalar_tensor_tensor", Crow, Arow, -1.0, Brow,
                         ALU.mult, ALU.mult, inc="lse")
                sch.wait("tensor", ("lse", sch.n("lse")))
                for sg in range(ncols // 512):
                    sch.emit("tensor", "matmul", P1[:, 0:512], ones_r,
                             Brow[:, sg * 512:(sg + 1) * 512], start=True, stop=True,
                             waits=[("bce", sch.n("bce"))] if sg > 0 else [])
                    sch.emit("tensor", "matmul", P2[:, 0:512], ones_r,
                             Crow[:, sg * 512:(sg + 1) * 512], start=True, stop=True,
                             inc="bcm")
                    sch.emit("scalar", "activation", a32.ap(r_bcA, 0, 128, sg * 512, 512),
                             P1[:, 0:512], AF.Copy, waits=[("bcm", sch.n("bcm"))])
                    sch.emit("scalar", "activation", a32.ap(r_bcB, 0, 128, sg * 512, 512),
                             P2[:, 0:512], AF.Copy, inc="bce")

            fin_stats(r_acc, TT)
            for c in range(8):
                fs = "fua"
                accc = a32.ap(r_acc, 0, 128, c * TT, TT)
                t = a32.ap(r_ftmp, 0, 128, 0, TT)
                y1c = a32.ap(r_fu, 0, 128, 0, TT)
                sch.emit("sync", "dma_start", out=y1c,
                         in_=y1_dram[c * 128:(c + 1) * 128, tile * TT:(tile + 1) * TT],
                         waits=[("fcv", sch.n("fcv"))], inc=fs, incn=16)
                w = [("bce", sch.n("bce"))] if c == 0 else []
                sch.emit("vector", "tensor_mul", t, accc, a32.ap(r_bcA, 0, 128, 0, TT),
                         waits=w)
                sch.emit("vector", "tensor_add", t, t, a32.ap(r_bcB, 0, 128, 0, TT),
                         inc="fcv")
                sch.emit("scalar", "activation", accc, t, AF.Identity,
                         scale=pvec("g2", c), bias=pvec("be2", c),
                         waits=[("fcv", sch.n("fcv"))], inc="fcs")
                sch.wait("vector", ("fcs", sch.n("fcs")), (fs, 16 * (sch.n(fs) // 16)))
                sch.emit("vector", "tensor_add", accc, accc, y1c, inc="fcv")
            fin_stats(r_acc, TT)
            for c in range(8):
                accc = a32.ap(r_acc, 0, 128, c * TT, TT)
                t = a32.ap(r_ftmp, 0, 128, 0, TT)
                w = [("bce", sch.n("bce"))] if c == 0 else []
                sch.emit("vector", "tensor_mul", t, accc, a32.ap(r_bcA, 0, 128, 0, TT),
                         waits=w)
                sch.emit("vector", "tensor_add", accc, t, a32.ap(r_bcB, 0, 128, 0, TT),
                         inc="fcv")
                sch.emit("sync", "dma_start",
                         out=out_d[c * 128:(c + 1) * 128, tile * TT:(tile + 1) * TT],
                         in_=accc, waits=[("fcv", sch.n("fcv"))], inc="acs", incn=16)
        sch.wait("sync", ("acs", sch.n("acs")))

        sch.replay_all(block)

    nc._pool_live = dict(pool.live)
    nc._dbg = dict(DBG)
    nc._pool = pool
    n = sum(len(v) for v in sch.prog.values())
    print(f"kernel instructions: {n} "
          f"(tensor={len(sch.prog['tensor'])}, scalar={len(sch.prog['scalar'])}, "
          f"vector={len(sch.prog['vector'])}, gpsimd={len(sch.prog['gpsimd'])}, "
          f"sync={len(sch.prog['sync'])}) sems={len(sch.sems)} "
          f"hwm={pool.hwm}B")
    return nc


_NC_CACHE = {}


def _host_stage(inputs):
    import jax
    x = np.asarray(inputs["L0_attn_x"])
    with jax.default_device(jax.devices("cpu")[0]):
        noise = np.asarray(jax.random.normal(jax.random.key(42, impl="threefry2x32"),
                                             (B, S, E), dtype="float32"))

    dr = 128
    inv = 1.0 / (10000.0 ** (np.arange(0, dr, 2, dtype=np.float64) / dr))
    tpos = np.arange(S, dtype=np.float64)
    fr = tpos[:, None] * inv[None, :]
    emb = np.concatenate([fr, fr], axis=-1)
    cosT = np.cos(emb).astype(np.float32).T
    sinT = np.sin(emb).astype(np.float32).T

    par_order = np.concatenate([np.arange(0, 32, 2), np.arange(1, 32, 2)])
    perm = (par_order[:, None] * 128 + np.arange(128)[None, :]).reshape(-1)

    def pcols(v):
        v = np.asarray(v, np.float32).reshape(-1)
        ncol = (len(v) + 127) // 128
        return v.reshape(ncol, 128).T
    pv = np.zeros((128, PV_NCOL), np.float32)
    pmap = {"bq": "L0_attn_bq", "bqc": "L0_attn_bqc", "bqr": "L0_attn_bqr",
            "bkv": "L0_attn_bkv", "bkr": "L0_attn_bkr", "bkc": "L0_attn_bkc",
            "bv": "L0_attn_bv", "bo": "L0_attn_bo", "bdkv": "L0_ffn_bdkv",
            "buk": "L0_ffn_buk", "buv": "L0_ffn_buv", "bdq": "L0_ffn_bdq",
            "buq": "L0_ffn_buq", "bo2": "L0_ffn_bo", "g1": "L0_ffn_gamma1",
            "be1": "L0_ffn_beta1", "g2": "L0_ffn_gamma2", "be2": "L0_ffn_beta2"}
    for nme, src in pmap.items():
        cc = pcols(inputs[src])
        pv[:, PV_COL[nme]:PV_COL[nme] + cc.shape[1]] = cc

    wr = {}
    for nme, src in [("wq_r", "L0_attn_wq"), ("wqc_r", "L0_attn_wqc"),
                     ("wqr_r", "L0_attn_wqr"), ("wkv_r", "L0_attn_wkv"),
                     ("wkr_r", "L0_attn_wkr"), ("wkc_r", "L0_attn_wkc"),
                     ("wv_r", "L0_attn_wv"), ("wo_r", "L0_attn_wo"),
                     ("wdkv_r", "L0_ffn_Wdkv"), ("wdq_r", "L0_ffn_Wdq"),
                     ("wuk_r", "L0_ffn_Wuk"), ("wuv_r", "L0_ffn_Wuv"),
                     ("wuq_r", "L0_ffn_Wuq"), ("wo2_r", "L0_ffn_Wo"),
                     ("wg_r", "L0_ffn_Wg"), ("wn_r", "L0_ffn_Wn")]:
        wr[nme] = relayout_lhsT(np.asarray(inputs[src], np.float32))
    w1r = relayout_w1(np.asarray(inputs["L0_ffn_W1"], np.float32))
    w2r = relayout_w2(np.asarray(inputs["L0_ffn_W2"], np.float32))
    b1p = np.ascontiguousarray(
        np.asarray(inputs["L0_ffn_b1"], np.float32).reshape(E * MC, 128).T)
    b2v = np.asarray(inputs["L0_ffn_b2"], np.float32)
    bg8 = np.asarray(inputs["L0_ffn_bg"], np.float32).reshape(1, 8)
    bn8 = np.asarray(inputs["L0_ffn_bn"], np.float32).reshape(1, 8)

    tril = (np.tril(np.ones((128, 128), np.float32), -1) * NEGM)
    full = np.full((128, 128), NEGM, np.float32)
    zero = np.zeros((128, 128), np.float32)

    in_maps = []
    for c in range(8):
        b = c // 2
        h = c % 2
        xb = np.ascontiguousarray(x[b].T)
        xT_all = np.ascontiguousarray(xb[:, perm])
        own_chunks = par_order_own = np.arange(h, 32, 2)
        own_tok = (own_chunks[:, None] * 128 + np.arange(128)[None, :]).reshape(-1)
        xT_own = np.ascontiguousarray(xb[:, own_tok])
        sswapT = np.concatenate([sinT[64:128], sinT[0:64]], axis=0)
        cs_q = np.stack([np.ascontiguousarray(cosT[:, own_tok]),
                         np.ascontiguousarray(sswapT[:, own_tok])])
        cs_k = np.stack([np.ascontiguousarray(cosT[:, perm]),
                         np.ascontiguousarray(sswapT[:, perm])])
        if h == 0:
            m0, m1 = tril, full
        else:
            m0, m1 = zero, tril
        ntm = noise[b][own_tok].reshape(NQC, 128, E).transpose(1, 0, 2).reshape(128, NQC * E)
        im = {"xT_own": xT_own, "xT_all": xT_all, "cs_q": cs_q, "cs_k": cs_k,
              "mh0": np.ascontiguousarray(m0), "mh1": np.ascontiguousarray(m1),
              "noise_tm": np.ascontiguousarray(ntm), "pvec": pv, "bg8": bg8, "bn8": bn8,
              "b2": b2v, "b1p": b1p, "w1r": w1r, "w2r": w2r}
        im.update(wr)
        in_maps.append(im)
    return in_maps


def kernel(**inputs):
    in_maps = _host_stage(inputs)
    if "nc" not in _NC_CACHE:
        _NC_CACHE["nc"] = build()
    nc = _NC_CACHE["nc"]
    res = run_bass_kernel_spmd(nc, in_maps, core_ids=list(range(8)))
    y = np.zeros((B, S, D), np.float32)
    for c in range(8):
        b = c // 2
        h = c % 2
        o = res.results[c]["out"]
        own_chunks = np.arange(h, 32, 2)
        for j, g in enumerate(own_chunks):
            y[b, g * 128:(g + 1) * 128, :] = o[:, j * 128:(j + 1) * 128].T
    return y
